# version 07
# baseline (speedup 1.0000x reference)
import numpy as np
import concourse.bass as bass
import concourse.mybir as mybir
from concourse.bass_utils import run_bass_kernel_spmd

F32 = mybir.dt.float32
BF16 = mybir.dt.bfloat16
AF = mybir.ActivationFunctionType
ALU = mybir.AluOpType

L, DM, H, NS, RK = 4096, 1024, 2048, 16, 64
T = 256
NCH = L // T
E = T + 2
KD = DM // 128
KH = H // 128
NE = 2 * H // 128

ENGS = ("T", "V", "S", "G", "Q")


def bap(ap, reps, inner):
    return bass.AP(ap.tensor, ap.offset, [list(ap.ap[0]), [0, reps], [1, inner]])


def flat(handle_or_ap):
    ap = handle_or_ap.ap() if hasattr(handle_or_ap, "ap") and not isinstance(
        handle_or_ap, bass.AP) else handle_or_ap
    return ap.rearrange("p a t -> p (a t)")


class Prog:
    def __init__(self):
        self.ops = {e: [] for e in ENGS}
        self.cnt = {}
        self.hw = {}

    def snap(self, sem):
        return (sem, self.cnt.get(sem, 0))

    def add(self, eng, fn, deps=(), inc=0, sem=None):
        sem = sem or eng
        self.ops[eng].append((fn, tuple(deps), inc, sem))
        if inc:
            self.cnt[sem] = self.cnt.get(sem, 0) + inc
        return self.snap(sem)

    def compute_needed(self):
        needed = {}
        for eng in ENGS:
            for fn, deps, inc, sem in self.ops[eng]:
                for (de, dv) in deps:
                    if de != eng or eng == "Q":
                        needed.setdefault(de, set()).add(dv)
        for sem, tot in self.cnt.items():
            needed.setdefault(sem, set()).add(tot)
        self.needed = needed

    def emit(self, eng, handle, sems):
        if not hasattr(self, "needed"):
            self.compute_needed()
        run = {}
        pend = {}
        for fn, deps, inc, sem in self.ops[eng]:
            for (de, dv) in deps:
                if dv <= 0:
                    continue
                if de == eng and eng != "Q":
                    continue
                key = (eng, de)
                if self.hw.get(key, -1) >= dv:
                    continue
                self.hw[key] = dv
                handle.wait_ge(sems[de], dv)
            instr = fn(handle)
            if inc:
                assert instr is not None, "milestone op returned no instruction"
                run[sem] = run.get(sem, 0) + inc
                pend[sem] = pend.get(sem, 0) + inc
                if eng in ("V", "S", "G"):
                    if run[sem] in self.needed.get(sem, ()):
                        handle.drain(fusable=True).then_inc(sems[sem], pend[sem])
                        pend[sem] = 0
                else:
                    instr.then_inc(sems[sem], inc)
        for sem, p in pend.items():
            if p:
                handle.drain(fusable=True).then_inc(sems[sem], p)


def build_nc():
    nc = bass.Bass("TRN2", target_bir_lowering=False)

    xT = nc.dram_tensor("xT", [DM, L], F32, kind="ExternalInput").ap()
    w_inT = nc.dram_tensor("w_inT", [DM, 2 * H], F32, kind="ExternalInput").ap()
    convw = nc.dram_tensor("convw", [H, 3], F32, kind="ExternalInput").ap()
    convb = nc.dram_tensor("convb", [H], F32, kind="ExternalInput").ap()
    w_xprojT = nc.dram_tensor("w_xprojT", [H, 96], F32, kind="ExternalInput").ap()
    w_dtT = nc.dram_tensor("w_dtT", [RK, H], F32, kind="ExternalInput").ap()
    b_dt = nc.dram_tensor("b_dt", [H], F32, kind="ExternalInput").ap()
    A_log = nc.dram_tensor("A_log", [NS], F32, kind="ExternalInput").ap()
    Dvec = nc.dram_tensor("Dvec", [H], F32, kind="ExternalInput").ap()
    w_outT = nc.dram_tensor("w_outT", [H, DM], F32, kind="ExternalInput").ap()
    w_rT = nc.dram_tensor("w_rT", [DM, DM], F32, kind="ExternalInput").ap()
    w_kT = nc.dram_tensor("w_kT", [DM, DM], F32, kind="ExternalInput").ap()
    w_vT = nc.dram_tensor("w_vT", [DM, DM], F32, kind="ExternalInput").ap()
    mu_r = nc.dram_tensor("mu_r", [DM], F32, kind="ExternalInput").ap()
    mu_k = nc.dram_tensor("mu_k", [DM], F32, kind="ExternalInput").ap()
    outT = nc.dram_tensor("outT", [DM, L], F32, kind="ExternalOutput").ap()

    wsin = nc.dram_tensor("wsin", [128, NE, 2, KD, 128], BF16, kind="Internal").ap()
    wsout = nc.dram_tensor("wsout", [128, KD, 2, KH, 128], BF16, kind="Internal").ap()
    wsr = nc.dram_tensor("wsr", [128, KD, KD, 128], BF16, kind="Internal").ap()
    wsk = nc.dram_tensor("wsk", [128, KD, KD, 128], BF16, kind="Internal").ap()
    wsv = nc.dram_tensor("wsv", [128, KD, KD, 128], BF16, kind="Internal").ap()
    dscrB = nc.dram_tensor("dscrB", [NS, T], BF16, kind="Internal").ap()
    dscrC = nc.dram_tensor("dscrC", [NS, T], F32, kind="Internal").ap()

    A = nc.alloc_sbuf_tensor
    sb_wxp = A("sb_wxp", [128, KH, 96], F32)
    sb_wdt = A("sb_wdt", [RK, KH, 128], BF16)
    sb_convw = A("sb_convw", [128, KH, 3], F32)
    sb_convb = A("sb_convb", [128, KH], F32)
    sb_bdt = A("sb_bdt", [128, KH], F32)
    sb_D = A("sb_D", [128, KH], F32)
    sb_mur = A("sb_mur", [128, KD], F32)
    sb_muk = A("sb_muk", [128, KD], F32)
    sb_omur = A("sb_omur", [128, KD], F32)
    sb_omuk = A("sb_omuk", [128, KD], F32)
    sb_Arep = A("sb_Arep", [128, NS], F32)
    sb_Alog = A("sb_Alog", [1, NS], F32)
    sb_Aflat = A("sb_Aflat", [1, NS], F32)
    ones_r = A("ones_r", [1, 128], F32)
    ones_c = A("ones_c", [128, 1], F32)
    ident = A("ident", [128, 128], F32)
    ring_in = A("ring_in", [128, 2, 2, KD, 128], BF16)
    ring_out = A("ring_out", [128, 2, 2, KH, 128], BF16)
    ring_ffn = A("ring_ffn", [128, 2, KD, 128], BF16)
    x_ext = A("x_ext", [128, KD, E], F32)
    xbf = A("xbf", [128, KD, E], BF16)
    xlo = A("xlo", [128, KD, E], BF16)
    u_ext = A("u_ext", [128, KH, E], F32)
    sz = A("sz", [128, KH, T], F32)
    uc = A("uc", [128, KH, T], F32)
    ucbf = A("ucbf", [128, KH, T], BF16)
    dtb = A("dtb", [128, KH, T], F32)
    ybuf = A("ybuf", [128, KH, T], F32)
    S1 = A("S1", [128, KH, T], F32)
    S2 = A("S2", [128, KH, T], BF16)
    S3 = A("S3", [128, KH, T], F32)
    Bb = [A("Bba", [128, T], BF16), A("Bbb", [128, T], BF16)]
    Cb = [A("Cba", [128, T], F32), A("Cbb", [128, T], F32)]
    hlast = A("hlast", [128, NS, KH], F32)
    dtpbf = A("dtpbf", [RK, T], BF16)
    BCf32 = A("BCf32", [32, T], F32)
    BCsbB = A("BCsbB", [NS, T], BF16)
    x1 = A("x1", [128, KD, 1 + T], F32)
    xbcol = A("xbcol", [128, KD, 1], F32)
    rowm = A("rowm", [1, T], F32)
    rowv = A("rowv", [1, T], F32)
    rowt = A("rowt", [1, T], F32)
    TMv = u_ext.ap()[:, :, 0:T]
    wtmp = ybuf.ap()[:, :, 0:128]
    wtmpb = ucbf.ap()[:, :, 0:128]

    PA = [nc.alloc_psum_tensor(f"PA{i}", [128, 512], F32) for i in range(4)]
    PB = nc.alloc_psum_tensor("PB", [128, 512], F32)
    PC = nc.alloc_psum_tensor("PC", [128, 512], F32)
    PD = nc.alloc_psum_tensor("PD", [128, 512], F32)
    PE2 = nc.alloc_psum_tensor("PE2", [128, 512], F32)
    PBANKS = PA + [PB, PC, PD, PE2]

    qsems = (["QL", "QP", "QQ", "QX", "QS", "QB1"]
             + [f"QI{i}" for i in range(2)] + [f"QO{i}" for i in range(2)]
             + [f"QF{i}" for i in range(2)]
             + ["QB1C", "QBB0", "QBB1", "QCC0", "QCC1"])
    sems = {e: nc.alloc_semaphore("sem" + e) for e in list(ENGS) + qsems}
    P = Prog()

    def dma(dst, src, slow=False):
        if slow:
            return lambda q: q.dma_start(out=dst, in_=src,
                                         allow_slow_non_contiguous=True)
        return lambda q: q.dma_start(out=dst, in_=src)

    for dst, src in (
        (sb_convw[:], convw.rearrange("(k p) c -> p k c", p=128)),
        (sb_convb[:], convb.rearrange("(k p) -> p k", p=128)),
        (sb_bdt[:], b_dt.rearrange("(k p) -> p k", p=128)),
        (sb_D[:], Dvec.rearrange("(k p) -> p k", p=128)),
        (sb_mur[:], mu_r.rearrange("(k p) -> p k", p=128)),
        (sb_muk[:], mu_k.rearrange("(k p) -> p k", p=128)),
        (sb_Alog[:], bass.AP(A_log.tensor, 0, [[NS, 1], [1, NS]])),
    ):
        P.add("Q", dma(dst, src, slow=True), inc=16, sem="QL")
    q_small = P.snap("QL")

    P.add("V", lambda v: v.memset(ones_r[:], 1.0), inc=1)
    P.add("V", lambda v: v.memset(ones_c[:], 1.0), inc=1)
    P.add("V", lambda v: v.memset(hlast[:], 0.0), inc=1)
    P.add("V", lambda v: v.memset(x_ext[:, :, 0:1], 0.0), inc=1)
    P.add("V", lambda v: v.memset(x1[:, :, 0:1], 0.0), inc=1)
    v_init = P.snap("V")
    P.add("G", lambda g: g.memset(ident[:], 0.0), inc=1)
    P.add("G", lambda g: g.affine_select(
        out=ident[:], in_=ident[:], compare_op=ALU.not_equal, fill=1.0,
        base=0, pattern=[[-1, 128]], channel_multiplier=1), inc=1)
    g_ident = P.snap("G")

    s_ae = P.add("S", lambda s: s.activation(out=sb_Aflat[:], in_=sb_Alog[:],
                 func=AF.Exp), deps=[q_small], inc=1)
    v_aneg = P.add("V", lambda v: v.tensor_scalar(
        out=sb_Aflat[:], in0=sb_Aflat[:], scalar1=-1.0, scalar2=None,
        op0=ALU.mult), deps=[s_ae], inc=1)
    t_arep = P.add("T", lambda t: t.matmul(PB[0:128, 0:NS], ones_r[:], sb_Aflat[:],
                   start=True, stop=True), deps=[v_aneg, v_init], inc=1)
    v_arep = P.add("V", lambda v: v.tensor_copy(out=sb_Arep[:], in_=PB[0:128, 0:NS]),
                   deps=[t_arep], inc=1)
    pb_free = v_arep
    pc_free = ("V", 0)
    pd_free = ("V", 0)
    pe_free = ("V", 0)

    P.add("V", lambda v: v.tensor_scalar(out=sb_omur[:], in0=sb_mur[:],
          scalar1=-1.0, scalar2=1.0, op0=ALU.mult, op1=ALU.add), deps=[q_small], inc=1)
    P.add("V", lambda v: v.tensor_scalar(out=sb_omuk[:], in0=sb_muk[:],
          scalar1=-1.0, scalar2=1.0, op0=ALU.mult, op1=ALU.add), deps=[q_small], inc=1)

    q_wxp = P.add("Q", dma(sb_wxp[:], w_xprojT.rearrange("(k p) e -> p k e", p=128)),
                  inc=16, sem="QL")
    q_wdt = P.add("Q", dma(wtmp[0:RK, :, :],
                  w_dtT.rearrange("r (k p) -> r k p", k=KH)), inc=16, sem="QL")
    v_wdt = P.add("V", lambda v: v.tensor_copy(out=sb_wdt[:], in_=wtmp[0:RK, :, :]),
                  deps=[q_wdt], inc=1)

    pieces = []
    w_inT_r = w_inT.rearrange("(k p) e -> p k e", p=128)
    for et in range(NE):
        pieces.append((w_inT_r[:, :, et * 128:(et + 1) * 128], KD, wsin[:, et], True))
    w_outT_r = w_outT.rearrange("(k p) e -> p k e", p=128)
    for a in range(KD):
        pieces.append((w_outT_r[:, :, a * 128:(a + 1) * 128], KH, wsout[:, a], True))
    for (w, ws) in ((w_rT, wsr), (w_kT, wsk), (w_vT, wsv)):
        w_r = w.rearrange("(k p) e -> p k e", p=128)
        for a in range(KD):
            pieces.append((w_r[:, :, a * 128:(a + 1) * 128], KD, ws[:, a], False))

    prev_v = v_wdt
    prev_store = ("QQ", 0)
    for (src, ntile, dst, lo) in pieces:
        ld = P.add("Q", dma(wtmp[:, 0:ntile, :], src), deps=[prev_v], inc=16, sem="QP")
        if lo:
            P.add("V", (lambda nt: (lambda v: v.tensor_copy(
                out=wtmpb[:, 0:nt, :], in_=wtmp[:, 0:nt, :])))(ntile),
                deps=[ld, prev_store], inc=1)
            h_v = P.snap("V")
            hst = P.add("Q", (lambda d, nt: (lambda q: q.dma_start(
                out=d[:, 0], in_=wtmpb[:, 0:nt, :])))(dst, ntile),
                deps=[h_v], inc=16, sem="QQ")
            P.add("V", (lambda nt: (lambda v: v.tensor_tensor(
                out=wtmpb[:, 0:nt, :], in0=wtmp[:, 0:nt, :],
                in1=wtmpb[:, 0:nt, :], op=ALU.subtract)))(ntile),
                deps=[hst], inc=1)
            prev_v = P.snap("V")
            prev_store = P.add("Q", (lambda d, nt: (lambda q: q.dma_start(
                out=d[:, 1], in_=wtmpb[:, 0:nt, :])))(dst, ntile),
                deps=[prev_v], inc=16, sem="QQ")
        else:
            P.add("V", (lambda nt: (lambda v: v.tensor_copy(
                out=wtmpb[:, 0:nt, :], in_=wtmp[:, 0:nt, :])))(ntile),
                deps=[ld, prev_store], inc=1)
            prev_v = P.snap("V")
            prev_store = P.add("Q", (lambda d, nt: (lambda q: q.dma_start(
                out=d, in_=wtmpb[:, 0:nt, :])))(dst, ntile),
                deps=[prev_v], inc=16, sem="QQ")
    q_wbig = P.snap("QQ")

    last_x_reader = ("V", 0)
    last_uext_V = ("V", 0)
    last_uext_T = ("T", 0)
    last_sz_reader = ("V", 0)
    last_dt_readers = []
    last_yevac = ("V", 0)
    last_x1_readers = []
    last_uc_war = ("QS", 0)
    last_ucbf_T = ("T", 0)
    last_ucbf_G = ("G", 0)
    last_xbf_T = ("T", 0)
    last_bcsb_Q = ("QB1", 0)
    last_dscr_Q = []
    s1_free = ("V", 0)
    s2_free = [("V", 0), ("T", 0)]
    s3_free = ("V", 0)
    bb_free = [("V", 0), ("V", 0)]
    cb_free = [("V", 0), ("V", 0)]
    pa_free = [v_init] * 4
    ring_in_free = [("T", 0)] * 2
    ring_out_free = [("T", 0)] * 2
    ring_ffn_free = [("T", 0)] * 2

    xTr = xT.rearrange("(k p) t -> p k t", p=128)
    outTr = outT.rearrange("(k p) t -> p k t", p=128)

    def emit_ln(src_of, pre_ms, dst_of, pbf, pcf, pdf, pef, s1f):
        t_sx = P.add("T", lambda t: [t.matmul(
            PD[0:1, 0:T], ones_c[:], src_of(a), start=(a == 0),
            stop=(a == KD - 1)) for a in range(KD)][-1],
            deps=pre_ms + [pdf], inc=1)
        sq_ms = []
        for a in range(KD):
            sq_ms.append(P.add("S", (lambda a_: (lambda s: s.activation(
                out=S1[:, a_, :], in_=src_of(a_), func=AF.Square)))(a),
                deps=pre_ms + [s1f], inc=1))
        t_sq = P.add("T", lambda t: [t.matmul(
            PE2[0:1, 0:T], ones_c[:], S1[:, a, :], start=(a == 0),
            stop=(a == KD - 1)) for a in range(KD)][-1],
            deps=sq_ms + [pef], inc=1)
        P.add("V", lambda v: v.tensor_scalar(out=rowm[:], in0=PD[0:1, 0:T],
              scalar1=1.0 / DM, scalar2=None, op0=ALU.mult), deps=[t_sx], inc=1)
        npdf = P.snap("V")
        P.add("V", lambda v: v.tensor_scalar(out=rowv[:], in0=PE2[0:1, 0:T],
              scalar1=1.0 / DM, scalar2=1e-5, op0=ALU.mult, op1=ALU.add),
              deps=[t_sq], inc=1)
        npef = P.snap("V")
        P.add("V", lambda v: v.tensor_tensor(out=rowt[:], in0=rowm[:],
              in1=rowm[:], op=ALU.mult), inc=1)
        v_var = P.add("V", lambda v: v.tensor_tensor(out=rowv[:], in0=rowv[:],
              in1=rowt[:], op=ALU.subtract), inc=1)
        P.add("S", lambda s: s.activation(out=rowt[:], in_=rowv[:],
              func=AF.Ln), deps=[v_var], inc=1)
        s_rstd = P.add("S", lambda s: s.activation(out=rowt[:], in_=rowt[:],
              func=AF.Exp, scale=-0.5), inc=1)
        t_mrep = P.add("T", lambda t: t.matmul(PB[:, 0:T], ones_r[:], rowm[:],
              start=True, stop=True), deps=[v_var, pbf], inc=1)
        t_rrep = P.add("T", lambda t: t.matmul(PC[:, 0:T], ones_r[:], rowt[:],
              start=True, stop=True), deps=[s_rstd, pcf], inc=1)
        outs = []
        npbf = None
        for a in range(KD):
            P.add("V", (lambda a_: (lambda v: v.tensor_tensor(
                out=S1[:, a_, :], in0=src_of(a_), in1=PB[:, 0:T],
                op=ALU.subtract)))(a), deps=[t_mrep], inc=1)
            if a == KD - 1:
                npbf = P.snap("V")
            outs.append(P.add("V", (lambda a_: (lambda v: v.tensor_tensor(
                out=dst_of(a_), in0=S1[:, a_, :], in1=PC[:, 0:T],
                op=ALU.mult)))(a), deps=[t_rrep], inc=1))
        return outs, npbf, outs[-1], npdf, npef

    for j in range(NCH):
        s0 = j * T

        if j == 0:
            q_x = P.add("Q", dma(x_ext[:, :, 1:E], xTr[:, :, 0:T + 1]),
                        deps=[last_x_reader, q_wbig], inc=16, sem="QX")
        elif j == NCH - 1:
            q_x = P.add("Q", dma(x_ext[:, :, 0:E - 1], xTr[:, :, s0 - 1:s0 + T]),
                        deps=[last_x_reader], inc=16, sem="QX")
        else:
            q_x = P.add("Q", dma(x_ext[:, :, 0:E], xTr[:, :, s0 - 1:s0 + T + 1]),
                        deps=[last_x_reader], inc=16, sem="QX")
        if j == NCH - 1:
            P.add("V", lambda v: v.memset(x_ext[:, :, E - 1:E], 0.0),
                  deps=[last_x_reader], inc=1)
        P.add("V", lambda v: v.tensor_copy(out=xbf[:], in_=x_ext[:]),
              deps=[q_x, last_xbf_T], inc=1)
        v_xbf = P.add("V", lambda v: v.scalar_tensor_tensor(
            out=xlo[:], in0=xbf[:], scalar=-1.0, in1=x_ext[:],
            op0=ALU.mult, op1=ALU.add), deps=[q_x, last_xbf_T], inc=1)

        q_wing = {}
        for g in range(2):
            q_wing[g] = P.add("Q", (lambda g_: (lambda q: q.dma_start(
                out=ring_in[:, g_ % 2], in_=wsin[:, g_])))(g),
                deps=[ring_in_free[g % 2]], inc=16, sem=f"QI{g % 2}")
        v_uevac = []
        s_zevac = []
        t_xz_last = None
        for et in range(NE):
            bank = et % 4
            slot = et % 2

            def mk_xz(et_, bank_, slot_):
                def f(t):
                    ins = None
                    for dk in range(KD):
                        ins = t.matmul(PA[bank_][:, 0:E],
                                       ring_in[:, slot_, 0, dk, :],
                                       xbf[:, dk, :], start=(dk == 0), stop=False)
                    for dk in range(KD):
                        ins = t.matmul(PA[bank_][:, 0:E],
                                       ring_in[:, slot_, 0, dk, :],
                                       xlo[:, dk, :], start=False, stop=False)
                    for dk in range(KD):
                        ins = t.matmul(PA[bank_][:, 0:E],
                                       ring_in[:, slot_, 1, dk, :],
                                       xbf[:, dk, :], start=False,
                                       stop=(dk == KD - 1))
                    return ins
                return f
            t_ms = P.add("T", mk_xz(et, bank, slot),
                         deps=[v_xbf, q_wing[et], pa_free[bank]], inc=1)
            ring_in_free[slot] = t_ms
            t_xz_last = t_ms
            if et + 2 < NE:
                q_wing[et + 2] = P.add("Q", (lambda g_: (lambda q: q.dma_start(
                    out=ring_in[:, g_ % 2], in_=wsin[:, g_])))(et + 2),
                    deps=[t_ms], inc=16, sem=f"QI{(et + 2) % 2}")
            if et < KH:
                deps = [t_ms]
                if et == 0:
                    deps += [last_uext_V, last_uext_T]
                    if j == 0:
                        deps += [q_wbig]
                ev = P.add("V", (lambda et_, bank_: (lambda v: v.tensor_copy(
                    out=u_ext[:, et_, :], in_=PA[bank_][:, 0:E])))(et, bank),
                    deps=deps, inc=1)
                v_uevac.append(ev)
                pa_free[bank] = ev
            else:
                deps = [t_ms]
                if et == KH:
                    deps += [last_sz_reader]
                ev = P.add("S", (lambda et_, bank_: (lambda s: s.activation(
                    out=sz[:, et_ - KH, :], in_=PA[bank_][:, 1:T + 1],
                    func=AF.Silu)))(et, bank), deps=deps, inc=1)
                s_zevac.append(ev)
                pa_free[bank] = ev
        last_xbf_T = t_xz_last

        conv_V = None
        for ct in range(KH):
            deps = [v_uevac[ct]]
            if ct == 0:
                deps = deps + last_dt_readers + [last_uc_war]

            def mk_conv(ct_):
                def f(e):
                    e.tensor_scalar(out=dtb[:, ct_, :], in0=u_ext[:, ct_, 0:T],
                                    scalar1=sb_convw[:, ct_, 0:1],
                                    scalar2=sb_convb[:, ct_:ct_ + 1],
                                    op0=ALU.mult, op1=ALU.add)
                    e.scalar_tensor_tensor(out=dtb[:, ct_, :],
                                           in0=u_ext[:, ct_, 1:T + 1],
                                           scalar=sb_convw[:, ct_, 1:2],
                                           in1=dtb[:, ct_, :],
                                           op0=ALU.mult, op1=ALU.add)
                    return e.scalar_tensor_tensor(
                        out=dtb[:, ct_, :], in0=u_ext[:, ct_, 2:T + 2],
                        scalar=sb_convw[:, ct_, 2:3], in1=dtb[:, ct_, :],
                        op0=ALU.mult, op1=ALU.add)
                return f
            conv_V = P.add("V", mk_conv(ct), deps=deps, inc=1)
        s_uc = P.add("S", lambda s: s.activation(out=flat(uc), in_=flat(dtb),
                     func=AF.Silu), deps=[conv_V, last_uc_war], inc=1)
        v_ucbf = P.add("V", lambda v: v.tensor_copy(out=ucbf[:], in_=uc[:]),
                       deps=[s_uc, last_ucbf_T, last_ucbf_G] +
                       ([q_wbig] if j == 0 else []), inc=1)

        def mk_xdbl(t):
            ins = None
            for kt in range(KH):
                ins = t.matmul(PB[0:96, 0:T], sb_wxp[:, kt, :], uc[:, kt, :],
                               start=(kt == 0), stop=(kt == KH - 1))
            return ins
        t_xdbl = P.add("T", mk_xdbl, deps=[s_uc, q_wxp, pb_free], inc=1)
        v_dtp = P.add("V", lambda v: v.tensor_copy(out=dtpbf[:], in_=PB[0:RK, 0:T]),
                      deps=[t_xdbl], inc=1)
        P.add("V", lambda v: v.tensor_copy(out=BCf32[:], in_=PB[64:96, 0:T]),
              deps=[t_xdbl, last_bcsb_Q], inc=1)
        v_bc = P.add("V", lambda v: v.tensor_copy(out=BCsbB[:], in_=BCf32[0:NS, :]),
                     deps=[], inc=1)
        pb_free = v_bc
        q_bc1 = P.add("Q", dma(dscrB[:], BCsbB[:]),
                      deps=[v_bc] + last_dscr_Q, inc=16, sem="QB1")
        q_bc1c = P.add("Q", dma(dscrC[:], BCf32[NS:32, :]),
                       deps=[v_bc] + last_dscr_Q, inc=16, sem="QB1C")
        last_bcsb_Q = q_bc1c

        ev_last = None
        for ht in range(KH):
            bank = ht % 4
            t_ms = P.add("T", (lambda ht_, bank_: (lambda t: t.matmul(
                PA[bank_][:, 0:T], sb_wdt[:, ht_, :], dtpbf[:],
                start=True, stop=True)))(ht, bank),
                deps=[v_dtp, v_wdt, pa_free[bank]], inc=1)
            deps = [t_ms]
            if ht == 0:
                deps += [last_yevac]
            ev_last = P.add("V", (lambda ht_, bank_: (lambda v: v.tensor_scalar(
                out=ybuf[:, ht_, :], in0=PA[bank_][:, 0:T],
                scalar1=sb_bdt[:, ht_:ht_ + 1], scalar2=None, op0=ALU.add)))(ht, bank),
                deps=deps, inc=1)
            pa_free[bank] = ev_last
        v_dtpre = ev_last
        P.add("V", lambda v: v.tensor_scalar(out=flat(dtb), in0=flat(ybuf),
              scalar1=0.0, scalar2=None, op0=ALU.max),
              deps=[v_dtpre, conv_V, s_uc], inc=1)
        P.add("V", lambda v: v.tensor_scalar(out=flat(S1), in0=flat(ybuf),
              scalar1=0.0, scalar2=None, op0=ALU.min),
              deps=[v_dtpre, s1_free], inc=1)
        v_mn = P.snap("V")
        P.add("S", lambda s: s.activation(out=flat(S2), in_=flat(S1),
              func=AF.Exp), deps=[v_mn, s2_free[0], s2_free[1]], inc=1)
        P.add("S", lambda s: s.activation(out=flat(S1), in_=flat(S2),
              func=AF.Ln, bias=1.0), inc=1)
        s_spl = P.snap("S")
        s2_free = [s_spl, s2_free[1]]
        v_dt = P.add("V", lambda v: v.tensor_tensor(out=flat(dtb), in0=flat(dtb),
              in1=flat(S1), op=ALU.add), deps=[s_spl], inc=1)

        last_dt_readers = []
        last_dscr_Q = []
        t_imm_last = None
        v_scan_prev = ("V", 0)
        for n in range(NS):
            par = n % 2
            q_bb = P.add("Q", (lambda n_, par_: (lambda q: q.dma_start(
                out=Bb[par_][:],
                in_=bass.AP(dscrB.tensor, n_ * T, [[0, 128], [1, T]]))))(n, par),
                deps=[q_bc1, bb_free[par]], inc=16, sem=f"QBB{par}")
            q_cc = P.add("Q", (lambda n_, par_: (lambda q: q.dma_start(
                out=Cb[par_][:],
                in_=bass.AP(dscrC.tensor, n_ * T, [[0, 128], [1, T]]))))(n, par),
                deps=[q_bc1c, cb_free[par]], inc=16, sem=f"QCC{par}")
            last_dscr_Q += [q_bb, q_cc]
            s_da = P.add("S", (lambda n_: (lambda s: s.activation(
                out=flat(S1), in_=flat(dtb), func=AF.Exp,
                scale=sb_Arep[:, n_:n_ + 1])))(n),
                deps=[v_dt, v_arep, v_scan_prev], inc=1)
            last_dt_readers.append(s_da)
            v_r = P.add("V", (lambda par_: (lambda v: v.tensor_tensor(
                out=S2[:], in0=ucbf[:], in1=bap(Bb[par_][:], KH, T),
                op=ALU.mult)))(par),
                deps=[q_bb, v_ucbf, s2_free[0], s2_free[1]], inc=1)
            bb_free[par] = v_r
            s2_free = [v_r, s2_free[1]]

            def mk_scan(n_):
                def f(v):
                    ins = None
                    for k in range(KH):
                        ins = v.tensor_tensor_scan(
                            out=S3[:, k, :], data0=S1[:, k, :], data1=S2[:, k, :],
                            initial=hlast[:, n_, k:k + 1],
                            op0=ALU.mult, op1=ALU.add)
                    return ins
                return f
            v_scan = P.add("V", mk_scan(n), deps=[s_da, s3_free], inc=1)
            v_scan_prev = v_scan
            P.add("V", (lambda n_: (lambda v: v.tensor_copy(
                out=hlast[:, n_, :],
                in_=S3[:, :, T - 1:T].rearrange("p a o -> p (a o)"))))(n), inc=1)
            v_term = P.add("G", (lambda par_: (lambda g: g.tensor_tensor(
                out=TMv, in0=S3[:], in1=bap(Cb[par_][:], KH, T),
                op=ALU.mult)))(par),
                deps=[q_cc, last_uext_T, conv_V, v_scan], inc=1)
            cb_free[par] = v_term
            s3_free = v_term

            def mk_imm(n_):
                def f(t):
                    ins = None
                    for k in range(8):
                        ins = t.matmul(
                            PBANKS[k][:, 0:512], ident[:],
                            bass.AP(TMv.tensor, 2 * k * E,
                                    [[KH * E, 128], [E, 2], [1, T]]),
                            start=(n_ == 0), stop=(n_ == NS - 1),
                            skip_group_check=True)
                    return ins
                return f
            imm_deps = [v_term, g_ident]
            if n == 0:
                imm_deps += list(pa_free) + [pb_free, pc_free, pd_free, pe_free]
            t_imm = P.add("T", mk_imm(n), deps=imm_deps, inc=1)
            last_uext_T = t_imm
            t_imm_last = t_imm
        for k in range(8):
            ev = P.add("V", (lambda k_: (lambda v: v.tensor_copy(
                out=ybuf[:, 2 * k_:2 * k_ + 2, :].rearrange("p a t -> p (a t)"),
                in_=PBANKS[k_][:, 0:512])))(k), deps=[t_imm_last], inc=1)
            if k < 4:
                pa_free[k] = ev
        last_yevac = P.snap("V")
        pb_free = pc_free = pd_free = pe_free = last_yevac
        last_uext_V = last_yevac

        v_yd = None
        for ct in range(KH):
            v_yd = P.add("V", (lambda ct_: (lambda v: v.scalar_tensor_tensor(
                out=ybuf[:, ct_, :], in0=uc[:, ct_, :],
                scalar=sb_D[:, ct_:ct_ + 1], in1=ybuf[:, ct_, :],
                op0=ALU.mult, op1=ALU.add)))(ct), deps=[last_yevac, s_uc], inc=1)
        P.add("V", lambda v: v.tensor_tensor(
            out=flat(ybuf), in0=flat(ybuf), in1=flat(sz), op=ALU.mult),
            deps=[v_yd, s_zevac[-1]], inc=1)
        v_yfin = P.add("V", lambda v: v.tensor_copy(out=ucbf[:], in_=ybuf[:]),
                       deps=[last_ucbf_G, last_ucbf_T, t_xdbl], inc=1)
        v_ylo = P.add("V", lambda v: v.scalar_tensor_tensor(
            out=flat(S2), in0=flat(ucbf), scalar=-1.0, in1=flat(ybuf),
            op0=ALU.mult, op1=ALU.add), deps=[s2_free[0], s2_free[1]], inc=1)
        s2_free = [v_ylo, s2_free[1]]
        last_sz_reader = v_yfin

        q_wout = {}
        for g in range(2):
            q_wout[g] = P.add("Q", (lambda g_: (lambda q: q.dma_start(
                out=ring_out[:, g_ % 2], in_=wsout[:, g_])))(g),
                deps=[ring_out_free[g % 2]], inc=16, sem=f"QO{g % 2}")
        v_pre1 = []
        t_wout_last = None
        for a in range(KD):
            bank = a % 4
            slot = a % 2

            def mk_wout(a_, bank_, slot_):
                def f(t):
                    ins = None
                    for kt in range(KH):
                        ins = t.matmul(PA[bank_][:, 0:T], ring_out[:, slot_, 0, kt, :],
                                       ucbf[:, kt, :], start=(kt == 0), stop=False)
                    for kt in range(KH):
                        ins = t.matmul(PA[bank_][:, 0:T], ring_out[:, slot_, 0, kt, :],
                                       S2[:, kt, :], start=False, stop=False)
                    for kt in range(KH):
                        ins = t.matmul(PA[bank_][:, 0:T], ring_out[:, slot_, 1, kt, :],
                                       ucbf[:, kt, :], start=False,
                                       stop=(kt == KH - 1))
                    return ins
                return f
            t_ms = P.add("T", mk_wout(a, bank, slot),
                         deps=[v_yfin, v_ylo, q_wout[a], pa_free[bank]], inc=1)
            ring_out_free[slot] = t_ms
            t_wout_last = t_ms
            if a + 2 < KD:
                q_wout[a + 2] = P.add("Q", (lambda g_: (lambda q: q.dma_start(
                    out=ring_out[:, g_ % 2], in_=wsout[:, g_])))(a + 2),
                    deps=[t_ms], inc=16, sem=f"QO{a % 2}")
            ev = P.add("V", (lambda a_, bank_: (lambda v: v.scalar_tensor_tensor(
                out=ybuf[:, a_, :], in0=x_ext[:, a_, 1:T + 1], scalar=2.0,
                in1=PA[bank_][:, 0:T], op0=ALU.mult, op1=ALU.add)))(a, bank),
                deps=[t_ms, q_x], inc=1)
            pa_free[bank] = ev
            v_pre1.append(ev)
        last_x_reader = v_pre1[-1]
        last_ucbf_T = t_wout_last
        s2_free = [s2_free[0], t_wout_last]

        if j > 0:
            P.add("V", lambda v: v.tensor_copy(out=x1[:, :, 0:1], in_=xbcol[:]),
                  deps=[v_pre1[-1]] + last_x1_readers, inc=1)
        v_x1c0 = P.snap("V")
        ln1_outs, pb_free, pc_free, pd_free, pe_free = emit_ln(
            lambda a_: ybuf[:, a_, :],
            [v_pre1[-1]] + last_x1_readers,
            lambda a_: x1[:, a_, 1:T + 1], pb_free, pc_free, pd_free, pe_free,
            v_scan_prev)
        v_x1 = ln1_outs[-1]

        mix_ms = []
        for i_rk, (mu, omu, off) in enumerate(
                ((sb_mur, sb_omur, 0), (sb_muk, sb_omuk, 8))):
            for a in range(KD):
                def mk_mix(a_, mu_, omu_, off_):
                    def f(e):
                        e.tensor_scalar(out=S1[:, 0, :],
                                        in0=x1[:, a_, 1:T + 1],
                                        scalar1=mu_[:, a_:a_ + 1], scalar2=None,
                                        op0=ALU.mult)
                        return e.scalar_tensor_tensor(
                            out=ucbf[:, off_ + a_, :], in0=x1[:, a_, 0:T],
                            scalar=omu_[:, a_:a_ + 1], in1=S1[:, 0, :],
                            op0=ALU.mult, op1=ALU.add)
                    return f
                mix_ms.append(P.add("V", mk_mix(a, mu, omu, off),
                              deps=[v_x1, v_x1c0, v_yfin, t_wout_last], inc=1))
        v_xb = P.add("V", lambda v: v.tensor_copy(out=xbcol[:], in_=x1[:, :, T:T + 1]),
                     deps=[v_x1], inc=1)

        def ffn_dma(ws, base, g, deps):
            return P.add("Q", (lambda g_, ws_, b_: (lambda q: q.dma_start(
                out=ring_ffn[:, (b_ + g_) % 2, 0:KD, :], in_=ws_[:, g_])))(
                g, ws, base), deps=deps, inc=16, sem=f"QF{(base + g) % 2}")

        q_wr = {g: ffn_dma(wsr, 0, g, [ring_ffn_free[g % 2]]) for g in range(2)}
        q_wk = {}
        q_wv = {}
        rr_ms = []
        for a in range(KD):
            bank = a % 4
            t_ms = P.add("T", (lambda a_, bank_: (lambda t: [
                t.matmul(PA[bank_][:, 0:T], ring_ffn[:, a_ % 2, kt, :],
                         ucbf[:, kt, :], start=(kt == 0), stop=(kt == KD - 1))
                for kt in range(KD)][-1]))(a, bank),
                deps=mix_ms + [q_wr[a], pa_free[bank]], inc=1)
            ring_ffn_free[a % 2] = t_ms
            last_ucbf_T = t_ms
            if a + 2 < KD:
                q_wr[a + 2] = ffn_dma(wsr, 0, a + 2, [t_ms])
            else:
                q_wk[a + 2 - KD] = ffn_dma(wsk, KD, a + 2 - KD, [t_ms])
            ev = P.add("S", (lambda a_, bank_: (lambda s: s.activation(
                out=uc[:, a_, :], in_=PA[bank_][:, 0:T], func=AF.Sigmoid)))(a, bank),
                deps=[t_ms, v_yd, s_uc], inc=1)
            pa_free[bank] = ev
            rr_ms.append(ev)
        kact_ms = []
        for a in range(KD):
            bank = a % 4
            t_ms = P.add("T", (lambda a_, bank_: (lambda t: [
                t.matmul(PA[bank_][:, 0:T], ring_ffn[:, (KD + a_) % 2, kt, :],
                         ucbf[:, 8 + kt, :], start=(kt == 0), stop=(kt == KD - 1))
                for kt in range(KD)][-1]))(a, bank),
                deps=mix_ms + [q_wk[a], pa_free[bank]], inc=1)
            ring_ffn_free[a % 2] = t_ms
            last_ucbf_T = t_ms
            if a + 2 < KD:
                q_wk[a + 2] = ffn_dma(wsk, KD, a + 2, [t_ms])
            else:
                q_wv[a + 2 - KD] = ffn_dma(wsv, 2 * KD, a + 2 - KD, [t_ms])
            ev = P.add("V", (lambda a_, bank_: (lambda v: v.tensor_scalar(
                out=uc[:, 8 + a_, :], in0=PA[bank_][:, 0:T],
                scalar1=0.0, scalar2=None, op0=ALU.max)))(a, bank),
                deps=[t_ms, v_yd], inc=1)
            pa_free[bank] = ev
            kact_ms.append(P.add("S", (lambda a_: (lambda s: s.activation(
                out=xbf[:, a_, 0:T], in_=uc[:, 8 + a_, :], func=AF.Square)))(a),
                deps=[ev], inc=1))
        v_pre2 = []
        for a in range(KD):
            bank = a % 4
            t_ms = P.add("T", (lambda a_, bank_: (lambda t: [
                t.matmul(PA[bank_][:, 0:T], ring_ffn[:, a_ % 2, kt, :],
                         xbf[:, kt, 0:T], start=(kt == 0), stop=(kt == KD - 1))
                for kt in range(KD)][-1]))(a, bank),
                deps=kact_ms + [q_wv[a], pa_free[bank]], inc=1)
            ring_ffn_free[a % 2] = t_ms
            last_xbf_T = t_ms
            if a + 2 < KD:
                q_wv[a + 2] = ffn_dma(wsv, 2 * KD, a + 2, [t_ms])
            ev = P.add("V", (lambda a_, bank_: (lambda v: v.tensor_tensor(
                out=S3[:, a_, :], in0=PA[bank_][:, 0:T],
                in1=uc[:, a_, :], op=ALU.mult)))(a, bank),
                deps=[t_ms, rr_ms[a], s3_free] + mix_ms, inc=1)
            pa_free[bank] = ev
            v_pre2.append(P.add("G", (lambda a_: (lambda g: g.tensor_tensor(
                out=ybuf[:, 8 + a_, :], in0=S3[:, a_, :],
                in1=x1[:, a_, 1:T + 1], op=ALU.add)))(a), deps=[ev], inc=1))
        s3_free = v_pre2[-1]
        last_ucbf_G = ("G", 0)
        last_x1_readers = [v_pre2[-1], v_xb, mix_ms[-1]]

        ln2_outs, pb_free, pc_free, pd_free, pe_free = emit_ln(
            lambda a_: ybuf[:, 8 + a_, :],
            v_pre2 + [ln1_outs[-1], kact_ms[-1]],
            lambda a_: uc[:, a_, :], pb_free, pc_free, pd_free, pe_free,
            ln1_outs[-1])
        q_out = P.add("Q", (lambda j_: (lambda q: q.dma_start(
            out=outTr[:, :, j_ * T:(j_ + 1) * T],
            in_=uc[:, 0:KD, :])))(j), deps=[ln2_outs[-1]], inc=16, sem="QS")
        last_uc_war = q_out
        last_yevac = ln2_outs[-1]
        s1_free = ln2_outs[-1]

    totals = dict(P.cnt)

    block_cm = nc.Block()
    block = block_cm.__enter__()

    @block.sync
    def _(sync):
        P.emit("Q", sync, sems)
        for name, val in totals.items():
            if val > 0:
                sync.wait_ge(sems[name], val)

    @block.vector
    def _(v):
        P.emit("V", v, sems)

    @block.scalar
    def _(s):
        P.emit("S", s, sems)

    @block.gpsimd
    def _(g):
        P.emit("G", g, sems)

    @block.tensor
    def _(t):
        P.emit("T", t, sems)

    block_cm.__exit__(None, None, None)
    return nc


_NC_CACHE = None


def kernel(**inputs):
    global _NC_CACHE
    x = np.asarray(inputs["L0_attn_x"])
    B = x.shape[0]
    if _NC_CACHE is None:
        _NC_CACHE = build_nc()
    nc = _NC_CACHE

    shared = {
        "w_inT": np.ascontiguousarray(np.asarray(inputs["L0_attn_W_in"]).T),
        "convw": np.ascontiguousarray(np.asarray(inputs["L0_attn_conv_w"])[:, 0, :]),
        "convb": np.ascontiguousarray(np.asarray(inputs["L0_attn_conv_b"])),
        "w_xprojT": np.ascontiguousarray(np.asarray(inputs["L0_attn_W_xproj"]).T),
        "w_dtT": np.ascontiguousarray(np.asarray(inputs["L0_attn_W_dt"]).T),
        "b_dt": np.ascontiguousarray(np.asarray(inputs["L0_attn_b_dt"])),
        "A_log": np.ascontiguousarray(np.asarray(inputs["L0_attn_A_log"])),
        "Dvec": np.ascontiguousarray(np.asarray(inputs["L0_attn_D"])),
        "w_outT": np.ascontiguousarray(np.asarray(inputs["L0_attn_W_out"]).T),
        "w_rT": np.ascontiguousarray(np.asarray(inputs["L0_ffn_W_r"]).T),
        "w_kT": np.ascontiguousarray(np.asarray(inputs["L0_ffn_W_k"]).T),
        "w_vT": np.ascontiguousarray(np.asarray(inputs["L0_ffn_W_v"]).T),
        "mu_r": np.ascontiguousarray(np.asarray(inputs["L0_ffn_mu_r"])),
        "mu_k": np.ascontiguousarray(np.asarray(inputs["L0_ffn_mu_k"])),
    }
    in_maps = []
    for b in range(B):
        m = dict(shared)
        m["xT"] = np.ascontiguousarray(x[b].T)
        in_maps.append(m)
    res = run_bass_kernel_spmd(nc, in_maps, core_ids=list(range(B)))
    out = np.stack([np.asarray(res.results[b]["outT"]).T for b in range(B)], axis=0)
    return out.astype(np.float32)


# version 11
# speedup vs baseline: 1.0435x; 1.0435x over previous
import numpy as np
import concourse.bass as bass
import concourse.mybir as mybir
from concourse.bass_utils import run_bass_kernel_spmd

F32 = mybir.dt.float32
BF16 = mybir.dt.bfloat16
AF = mybir.ActivationFunctionType
ALU = mybir.AluOpType

L, DM, H, NS, RK = 4096, 1024, 2048, 16, 64
T = 256
NCH = L // T
E = T + 2
KD = DM // 128
KH = H // 128
NE = 2 * H // 128

ENGS = ("T", "V", "S", "G", "Q")


def bap(ap, reps, inner):
    return bass.AP(ap.tensor, ap.offset, [list(ap.ap[0]), [0, reps], [1, inner]])


def flat(handle_or_ap):
    ap = handle_or_ap.ap() if hasattr(handle_or_ap, "ap") and not isinstance(
        handle_or_ap, bass.AP) else handle_or_ap
    return ap.rearrange("p a t -> p (a t)")


class Prog:
    def __init__(self):
        self.ops = {e: [] for e in ENGS}
        self.cnt = {}
        self.hw = {}

    def snap(self, sem):
        return (sem, self.cnt.get(sem, 0))

    def add(self, eng, fn, deps=(), inc=0, sem=None):
        sem = sem or eng
        self.ops[eng].append((fn, tuple(deps), inc, sem))
        if inc:
            self.cnt[sem] = self.cnt.get(sem, 0) + inc
        return self.snap(sem)

    def compute_needed(self):
        needed = {}
        for eng in ENGS:
            for fn, deps, inc, sem in self.ops[eng]:
                for (de, dv) in deps:
                    if de != eng or eng == "Q":
                        needed.setdefault(de, set()).add(dv)
        for sem, tot in self.cnt.items():
            needed.setdefault(sem, set()).add(tot)
        self.needed = needed

    def emit(self, eng, handle, sems):
        if not hasattr(self, "needed"):
            self.compute_needed()
        run = {}
        pend = {}
        for fn, deps, inc, sem in self.ops[eng]:
            for (de, dv) in deps:
                if dv <= 0:
                    continue
                if de == eng and eng != "Q":
                    continue
                key = (eng, de)
                if self.hw.get(key, -1) >= dv:
                    continue
                self.hw[key] = dv
                handle.wait_ge(sems[de], dv)
            instr = fn(handle)
            if inc:
                assert instr is not None, "milestone op returned no instruction"
                run[sem] = run.get(sem, 0) + inc
                pend[sem] = pend.get(sem, 0) + inc
                if eng in ("V", "S", "G"):
                    if run[sem] in self.needed.get(sem, ()):
                        handle.drain(fusable=True).then_inc(sems[sem], pend[sem])
                        pend[sem] = 0
                else:
                    instr.then_inc(sems[sem], inc)
        for sem, p in pend.items():
            if p:
                handle.drain(fusable=True).then_inc(sems[sem], p)


def build_nc():
    nc = bass.Bass("TRN2", target_bir_lowering=False)

    xT = nc.dram_tensor("xT", [DM, L], F32, kind="ExternalInput").ap()
    w_inT = nc.dram_tensor("w_inT", [DM, 2 * H], F32, kind="ExternalInput").ap()
    convw = nc.dram_tensor("convw", [H, 3], F32, kind="ExternalInput").ap()
    convb = nc.dram_tensor("convb", [H], F32, kind="ExternalInput").ap()
    w_xprojT = nc.dram_tensor("w_xprojT", [H, 96], F32, kind="ExternalInput").ap()
    w_dtT = nc.dram_tensor("w_dtT", [RK, H], F32, kind="ExternalInput").ap()
    b_dt = nc.dram_tensor("b_dt", [H], F32, kind="ExternalInput").ap()
    A_log = nc.dram_tensor("A_log", [NS], F32, kind="ExternalInput").ap()
    Dvec = nc.dram_tensor("Dvec", [H], F32, kind="ExternalInput").ap()
    w_outT = nc.dram_tensor("w_outT", [H, DM], F32, kind="ExternalInput").ap()
    w_rT = nc.dram_tensor("w_rT", [DM, DM], F32, kind="ExternalInput").ap()
    w_kT = nc.dram_tensor("w_kT", [DM, DM], F32, kind="ExternalInput").ap()
    w_vT = nc.dram_tensor("w_vT", [DM, DM], F32, kind="ExternalInput").ap()
    mu_r = nc.dram_tensor("mu_r", [DM], F32, kind="ExternalInput").ap()
    mu_k = nc.dram_tensor("mu_k", [DM], F32, kind="ExternalInput").ap()
    outT = nc.dram_tensor("outT", [DM, L], F32, kind="ExternalOutput").ap()

    wsin = nc.dram_tensor("wsin", [128, NE, 2, KD, 128], BF16, kind="Internal").ap()
    wsout = nc.dram_tensor("wsout", [128, KD, 2, KH, 128], BF16, kind="Internal").ap()
    wsr = nc.dram_tensor("wsr", [128, KD, KD, 128], BF16, kind="Internal").ap()
    wsk = nc.dram_tensor("wsk", [128, KD, KD, 128], BF16, kind="Internal").ap()
    wsv = nc.dram_tensor("wsv", [128, KD, KD, 128], BF16, kind="Internal").ap()
    dscrB = nc.dram_tensor("dscrB", [NS, T], BF16, kind="Internal").ap()
    dscrC = nc.dram_tensor("dscrC", [NS, T], F32, kind="Internal").ap()

    A = nc.alloc_sbuf_tensor
    sb_wxp = A("sb_wxp", [128, KH, 96], F32)
    sb_wdt = A("sb_wdt", [RK, KH, 128], BF16)
    sb_convw = A("sb_convw", [128, KH, 3], F32)
    sb_convb = A("sb_convb", [128, KH], F32)
    sb_bdt = A("sb_bdt", [128, KH], F32)
    sb_D = A("sb_D", [128, KH], F32)
    sb_mur = A("sb_mur", [128, KD], F32)
    sb_muk = A("sb_muk", [128, KD], F32)
    sb_omur = A("sb_omur", [128, KD], F32)
    sb_omuk = A("sb_omuk", [128, KD], F32)
    sb_Arep = A("sb_Arep", [128, NS], F32)
    sb_Alog = A("sb_Alog", [1, NS], F32)
    sb_Aflat = A("sb_Aflat", [1, NS], F32)
    ones_r = A("ones_r", [1, 128], F32)
    ones_c = A("ones_c", [128, 1], F32)
    ident = A("ident", [128, 128], F32)
    ring_in = A("ring_in", [128, 2, 2, KD, 128], BF16)
    ring_out = A("ring_out", [128, 2, 2, KH, 128], BF16)
    ring_ffn = A("ring_ffn", [128, 2, KD, 128], BF16)
    x_ext = A("x_ext", [128, KD, E], F32)
    xbf = A("xbf", [128, KD, E], BF16)
    xlo = A("xlo", [128, KD, E], BF16)
    u_ext = A("u_ext", [128, KH, E], F32)
    sz = A("sz", [128, KH, T], F32)
    uc = A("uc", [128, KH, T], F32)
    ucbf = A("ucbf", [128, KH, T], BF16)
    dtb = A("dtb", [128, KH, T], F32)
    ybuf = A("ybuf", [128, KH, T], F32)
    S1 = A("S1", [128, KH, T], F32)
    S2 = A("S2", [128, KH, T], BF16)
    S3 = A("S3", [128, KH, T], F32)
    Bb = [A("Bba", [128, T], BF16), A("Bbb", [128, T], BF16)]
    Cb = [A("Cba", [128, T], F32), A("Cbb", [128, T], F32)]
    hlast = A("hlast", [128, NS, KH], F32)
    dtpbf = A("dtpbf", [RK, T], BF16)
    BCf32 = A("BCf32", [32, T], F32)
    BCsbB = A("BCsbB", [NS, T], BF16)
    x1 = A("x1", [128, KD, 1 + T], F32)
    xbcol = A("xbcol", [128, KD, 1], F32)
    rowm = A("rowm", [1, T], F32)
    rowv = A("rowv", [1, T], F32)
    rowt = A("rowt", [1, T], F32)
    TMv = u_ext.ap()[:, :, 0:T]
    wtmp = ybuf.ap()[:, :, 0:128]
    wtmpb = ucbf.ap()[:, :, 0:128]

    PA = [nc.alloc_psum_tensor(f"PA{i}", [128, 512], F32) for i in range(4)]
    PB = nc.alloc_psum_tensor("PB", [128, 512], F32)
    PC = nc.alloc_psum_tensor("PC", [128, 512], F32)
    PD = nc.alloc_psum_tensor("PD", [128, 512], F32)
    PE2 = nc.alloc_psum_tensor("PE2", [128, 512], F32)
    PBANKS = PA + [PB, PC, PD, PE2]

    qsems = (["QL", "QP", "QQ", "QX", "QS", "QB1"]
             + [f"QI{i}" for i in range(2)] + [f"QO{i}" for i in range(2)]
             + [f"QF{i}" for i in range(2)]
             + ["QB1C", "QBB0", "QBB1", "QCC0", "QCC1"])
    sems = {e: nc.alloc_semaphore("sem" + e) for e in list(ENGS) + qsems}
    P = Prog()

    def dma(dst, src, slow=False):
        if slow:
            return lambda q: q.dma_start(out=dst, in_=src,
                                         allow_slow_non_contiguous=True)
        return lambda q: q.dma_start(out=dst, in_=src)

    for dst, src in (
        (sb_convw[:], convw.rearrange("(k p) c -> p k c", p=128)),
        (sb_convb[:], convb.rearrange("(k p) -> p k", p=128)),
        (sb_bdt[:], b_dt.rearrange("(k p) -> p k", p=128)),
        (sb_D[:], Dvec.rearrange("(k p) -> p k", p=128)),
        (sb_mur[:], mu_r.rearrange("(k p) -> p k", p=128)),
        (sb_muk[:], mu_k.rearrange("(k p) -> p k", p=128)),
        (sb_Alog[:], bass.AP(A_log.tensor, 0, [[NS, 1], [1, NS]])),
    ):
        P.add("Q", dma(dst, src, slow=True), inc=16, sem="QL")
    q_small = P.snap("QL")

    P.add("V", lambda v: v.memset(ones_r[:], 1.0), inc=1)
    P.add("V", lambda v: v.memset(ones_c[:], 1.0), inc=1)
    P.add("V", lambda v: v.memset(hlast[:], 0.0), inc=1)
    P.add("V", lambda v: v.memset(x_ext[:, :, 0:1], 0.0), inc=1)
    P.add("V", lambda v: v.memset(x1[:, :, 0:1], 0.0), inc=1)
    v_init = P.snap("V")
    P.add("G", lambda g: g.memset(ident[:], 0.0), inc=1)
    P.add("G", lambda g: g.affine_select(
        out=ident[:], in_=ident[:], compare_op=ALU.not_equal, fill=1.0,
        base=0, pattern=[[-1, 128]], channel_multiplier=1), inc=1)
    g_ident = P.snap("G")

    s_ae = P.add("S", lambda s: s.activation(out=sb_Aflat[:], in_=sb_Alog[:],
                 func=AF.Exp), deps=[q_small], inc=1)
    v_aneg = P.add("V", lambda v: v.tensor_scalar(
        out=sb_Aflat[:], in0=sb_Aflat[:], scalar1=-1.0, scalar2=None,
        op0=ALU.mult), deps=[s_ae], inc=1)
    t_arep = P.add("T", lambda t: t.matmul(PB[0:128, 0:NS], ones_r[:], sb_Aflat[:],
                   start=True, stop=True), deps=[v_aneg, v_init], inc=1)
    v_arep = P.add("V", lambda v: v.tensor_copy(out=sb_Arep[:], in_=PB[0:128, 0:NS]),
                   deps=[t_arep], inc=1)
    pb_free = v_arep
    pc_free = ("V", 0)
    pd_free = ("V", 0)
    pe_free = ("V", 0)

    P.add("V", lambda v: v.tensor_scalar(out=sb_omur[:], in0=sb_mur[:],
          scalar1=-1.0, scalar2=1.0, op0=ALU.mult, op1=ALU.add), deps=[q_small], inc=1)
    P.add("V", lambda v: v.tensor_scalar(out=sb_omuk[:], in0=sb_muk[:],
          scalar1=-1.0, scalar2=1.0, op0=ALU.mult, op1=ALU.add), deps=[q_small], inc=1)

    q_wxp = P.add("Q", dma(sb_wxp[:], w_xprojT.rearrange("(k p) e -> p k e", p=128)),
                  inc=16, sem="QL")
    q_wdt = P.add("Q", dma(wtmp[0:RK, :, :],
                  w_dtT.rearrange("r (k p) -> r k p", k=KH)), inc=16, sem="QL")
    v_wdt = P.add("V", lambda v: v.tensor_copy(out=sb_wdt[:], in_=wtmp[0:RK, :, :]),
                  deps=[q_wdt], inc=1)

    pieces = []
    w_inT_r = w_inT.rearrange("(k p) e -> p k e", p=128)
    for et in range(NE):
        pieces.append((w_inT_r[:, :, et * 128:(et + 1) * 128], KD, wsin[:, et], True))
    w_outT_r = w_outT.rearrange("(k p) e -> p k e", p=128)
    for a in range(KD):
        pieces.append((w_outT_r[:, :, a * 128:(a + 1) * 128], KH, wsout[:, a], True))
    for (w, ws) in ((w_rT, wsr), (w_kT, wsk), (w_vT, wsv)):
        w_r = w.rearrange("(k p) e -> p k e", p=128)
        for a in range(KD):
            pieces.append((w_r[:, :, a * 128:(a + 1) * 128], KD, ws[:, a], False))

    prev_v = v_wdt
    prev_store = ("QQ", 0)
    for (src, ntile, dst, lo) in pieces:
        ld = P.add("Q", dma(wtmp[:, 0:ntile, :], src), deps=[prev_v], inc=16, sem="QP")
        if lo:
            P.add("V", (lambda nt: (lambda v: v.tensor_copy(
                out=wtmpb[:, 0:nt, :], in_=wtmp[:, 0:nt, :])))(ntile),
                deps=[ld, prev_store], inc=1)
            h_v = P.snap("V")
            hst = P.add("Q", (lambda d, nt: (lambda q: q.dma_start(
                out=d[:, 0], in_=wtmpb[:, 0:nt, :])))(dst, ntile),
                deps=[h_v], inc=16, sem="QQ")
            P.add("V", (lambda nt: (lambda v: v.tensor_tensor(
                out=wtmpb[:, 0:nt, :], in0=wtmp[:, 0:nt, :],
                in1=wtmpb[:, 0:nt, :], op=ALU.subtract)))(ntile),
                deps=[hst], inc=1)
            prev_v = P.snap("V")
            prev_store = P.add("Q", (lambda d, nt: (lambda q: q.dma_start(
                out=d[:, 1], in_=wtmpb[:, 0:nt, :])))(dst, ntile),
                deps=[prev_v], inc=16, sem="QQ")
        else:
            P.add("V", (lambda nt: (lambda v: v.tensor_copy(
                out=wtmpb[:, 0:nt, :], in_=wtmp[:, 0:nt, :])))(ntile),
                deps=[ld, prev_store], inc=1)
            prev_v = P.snap("V")
            prev_store = P.add("Q", (lambda d, nt: (lambda q: q.dma_start(
                out=d, in_=wtmpb[:, 0:nt, :])))(dst, ntile),
                deps=[prev_v], inc=16, sem="QQ")
    q_wbig = P.snap("QQ")

    last_x_reader = ("V", 0)
    last_uext_V = ("V", 0)
    last_uext_T = ("T", 0)
    last_sz_reader = ("V", 0)
    last_dt_readers = []
    last_yevac = ("V", 0)
    last_x1_readers = []
    last_uc_war = ("QS", 0)
    last_ucbf_T = ("T", 0)
    last_ucbf_G = ("G", 0)
    last_xbf_T = ("T", 0)
    last_bcsb_Q = ("QB1", 0)
    last_dscr_Q = []
    s1_free = ("V", 0)
    s2_free = [("V", 0), ("T", 0)]
    s3_free = ("V", 0)
    bb_free = [("V", 0), ("V", 0)]
    cb_free = [("V", 0), ("V", 0)]
    pa_free = [v_init] * 4
    ring_in_free = [("T", 0)] * 2
    ring_out_free = [("T", 0)] * 2
    ring_ffn_free = [("T", 0)] * 2

    xTr = xT.rearrange("(k p) t -> p k t", p=128)
    outTr = outT.rearrange("(k p) t -> p k t", p=128)

    def emit_ln(src_of, pre_ms, dst_of, pbf, pcf, pdf, pef, s1f):
        t_sx = P.add("T", lambda t: [t.matmul(
            PD[0:1, 0:T], ones_c[:], src_of(a), start=(a == 0),
            stop=(a == KD - 1)) for a in range(KD)][-1],
            deps=pre_ms + [pdf], inc=1)
        sq_ms = []
        for a in range(KD):
            sq_ms.append(P.add("S", (lambda a_: (lambda s: s.activation(
                out=S1[:, a_, :], in_=src_of(a_), func=AF.Square)))(a),
                deps=pre_ms + [s1f], inc=1))
        t_sq = P.add("T", lambda t: [t.matmul(
            PE2[0:1, 0:T], ones_c[:], S1[:, a, :], start=(a == 0),
            stop=(a == KD - 1)) for a in range(KD)][-1],
            deps=sq_ms + [pef], inc=1)
        P.add("V", lambda v: v.tensor_scalar(out=rowm[:], in0=PD[0:1, 0:T],
              scalar1=1.0 / DM, scalar2=None, op0=ALU.mult), deps=[t_sx], inc=1)
        npdf = P.snap("V")
        P.add("V", lambda v: v.tensor_scalar(out=rowv[:], in0=PE2[0:1, 0:T],
              scalar1=1.0 / DM, scalar2=1e-5, op0=ALU.mult, op1=ALU.add),
              deps=[t_sq], inc=1)
        npef = P.snap("V")
        P.add("V", lambda v: v.tensor_tensor(out=rowt[:], in0=rowm[:],
              in1=rowm[:], op=ALU.mult), inc=1)
        v_var = P.add("V", lambda v: v.tensor_tensor(out=rowv[:], in0=rowv[:],
              in1=rowt[:], op=ALU.subtract), inc=1)
        P.add("S", lambda s: s.activation(out=rowt[:], in_=rowv[:],
              func=AF.Ln), deps=[v_var], inc=1)
        s_rstd = P.add("S", lambda s: s.activation(out=rowt[:], in_=rowt[:],
              func=AF.Exp, scale=-0.5), inc=1)
        t_mrep = P.add("T", lambda t: t.matmul(PB[:, 0:T], ones_r[:], rowm[:],
              start=True, stop=True), deps=[v_var, pbf], inc=1)
        t_rrep = P.add("T", lambda t: t.matmul(PC[:, 0:T], ones_r[:], rowt[:],
              start=True, stop=True), deps=[s_rstd, pcf], inc=1)
        outs = []
        npbf = None
        for a in range(KD):
            P.add("V", (lambda a_: (lambda v: v.tensor_tensor(
                out=S1[:, a_, :], in0=src_of(a_), in1=PB[:, 0:T],
                op=ALU.subtract)))(a), deps=[t_mrep], inc=1)
            if a == KD - 1:
                npbf = P.snap("V")
            outs.append(P.add("V", (lambda a_: (lambda v: v.tensor_tensor(
                out=dst_of(a_), in0=S1[:, a_, :], in1=PC[:, 0:T],
                op=ALU.mult)))(a), deps=[t_rrep], inc=1))
        return outs, npbf, outs[-1], npdf, npef

    for j in range(NCH):
        s0 = j * T

        if j == 0:
            q_x = P.add("Q", dma(x_ext[:, :, 1:E], xTr[:, :, 0:T + 1]),
                        deps=[last_x_reader, q_wbig], inc=16, sem="QX")
        elif j == NCH - 1:
            q_x = P.add("Q", dma(x_ext[:, :, 0:E - 1], xTr[:, :, s0 - 1:s0 + T]),
                        deps=[last_x_reader], inc=16, sem="QX")
        else:
            q_x = P.add("Q", dma(x_ext[:, :, 0:E], xTr[:, :, s0 - 1:s0 + T + 1]),
                        deps=[last_x_reader], inc=16, sem="QX")
        if j == NCH - 1:
            P.add("V", lambda v: v.memset(x_ext[:, :, E - 1:E], 0.0),
                  deps=[last_x_reader], inc=1)
        P.add("V", lambda v: v.tensor_copy(out=xbf[:], in_=x_ext[:]),
              deps=[q_x, last_xbf_T], inc=1)
        v_xbf = P.add("V", lambda v: v.scalar_tensor_tensor(
            out=xlo[:], in0=xbf[:], scalar=-1.0, in1=x_ext[:],
            op0=ALU.mult, op1=ALU.add), deps=[q_x, last_xbf_T], inc=1)

        q_wing = {}
        for g in range(2):
            q_wing[g] = P.add("Q", (lambda g_: (lambda q: q.dma_start(
                out=ring_in[:, g_ % 2], in_=wsin[:, g_])))(g),
                deps=[ring_in_free[g % 2]], inc=16, sem=f"QI{g % 2}")
        v_uevac = []
        s_zevac = []
        t_xz_last = None
        for et in range(NE):
            bank = et % 4
            slot = et % 2

            def mk_xz(et_, bank_, slot_):
                def f(t):
                    ins = None
                    for dk in range(KD):
                        ins = t.matmul(PA[bank_][:, 0:E],
                                       ring_in[:, slot_, 0, dk, :],
                                       xbf[:, dk, :], start=(dk == 0), stop=False)
                    for dk in range(KD):
                        ins = t.matmul(PA[bank_][:, 0:E],
                                       ring_in[:, slot_, 0, dk, :],
                                       xlo[:, dk, :], start=False, stop=False)
                    for dk in range(KD):
                        ins = t.matmul(PA[bank_][:, 0:E],
                                       ring_in[:, slot_, 1, dk, :],
                                       xbf[:, dk, :], start=False,
                                       stop=(dk == KD - 1))
                    return ins
                return f
            t_ms = P.add("T", mk_xz(et, bank, slot),
                         deps=[v_xbf, q_wing[et], pa_free[bank]], inc=1)
            ring_in_free[slot] = t_ms
            t_xz_last = t_ms
            if et + 2 < NE:
                q_wing[et + 2] = P.add("Q", (lambda g_: (lambda q: q.dma_start(
                    out=ring_in[:, g_ % 2], in_=wsin[:, g_])))(et + 2),
                    deps=[t_ms], inc=16, sem=f"QI{(et + 2) % 2}")
            if et < KH:
                deps = [t_ms]
                if et == 0:
                    deps += [last_uext_V, last_uext_T]
                    if j == 0:
                        deps += [q_wbig]
                ev = P.add("V", (lambda et_, bank_: (lambda v: v.tensor_copy(
                    out=u_ext[:, et_, :], in_=PA[bank_][:, 0:E])))(et, bank),
                    deps=deps, inc=1)
                v_uevac.append(ev)
                pa_free[bank] = ev
            else:
                deps = [t_ms]
                if et == KH:
                    deps += [last_sz_reader]
                ev = P.add("S", (lambda et_, bank_: (lambda s: s.activation(
                    out=sz[:, et_ - KH, :], in_=PA[bank_][:, 1:T + 1],
                    func=AF.Silu)))(et, bank), deps=deps, inc=1)
                s_zevac.append(ev)
                pa_free[bank] = ev
        last_xbf_T = t_xz_last

        conv_V = None
        for ct in range(KH):
            deps = [v_uevac[ct]]
            if ct == 0:
                deps = deps + last_dt_readers + [last_uc_war]

            def mk_conv(ct_):
                def f(e):
                    e.tensor_scalar(out=dtb[:, ct_, :], in0=u_ext[:, ct_, 0:T],
                                    scalar1=sb_convw[:, ct_, 0:1],
                                    scalar2=sb_convb[:, ct_:ct_ + 1],
                                    op0=ALU.mult, op1=ALU.add)
                    e.scalar_tensor_tensor(out=dtb[:, ct_, :],
                                           in0=u_ext[:, ct_, 1:T + 1],
                                           scalar=sb_convw[:, ct_, 1:2],
                                           in1=dtb[:, ct_, :],
                                           op0=ALU.mult, op1=ALU.add)
                    return e.scalar_tensor_tensor(
                        out=dtb[:, ct_, :], in0=u_ext[:, ct_, 2:T + 2],
                        scalar=sb_convw[:, ct_, 2:3], in1=dtb[:, ct_, :],
                        op0=ALU.mult, op1=ALU.add)
                return f
            conv_V = P.add("V", mk_conv(ct), deps=deps, inc=1)
        s_uc = P.add("S", lambda s: s.activation(out=flat(uc), in_=flat(dtb),
                     func=AF.Silu), deps=[conv_V, last_uc_war], inc=1)
        v_ucbf = P.add("V", lambda v: v.tensor_copy(out=ucbf[:], in_=uc[:]),
                       deps=[s_uc, last_ucbf_T, last_ucbf_G] +
                       ([q_wbig] if j == 0 else []), inc=1)

        def mk_xdbl(t):
            ins = None
            for kt in range(KH):
                ins = t.matmul(PB[0:96, 0:T], sb_wxp[:, kt, :], uc[:, kt, :],
                               start=(kt == 0), stop=(kt == KH - 1))
            return ins
        t_xdbl = P.add("T", mk_xdbl, deps=[s_uc, q_wxp, pb_free], inc=1)
        v_dtp = P.add("V", lambda v: v.tensor_copy(out=dtpbf[:], in_=PB[0:RK, 0:T]),
                      deps=[t_xdbl], inc=1)
        P.add("V", lambda v: v.tensor_copy(out=BCf32[:], in_=PB[64:96, 0:T]),
              deps=[t_xdbl, last_bcsb_Q], inc=1)
        v_bc = P.add("V", lambda v: v.tensor_copy(out=BCsbB[:], in_=BCf32[0:NS, :]),
                     deps=[], inc=1)
        pb_free = v_bc
        q_bc1 = P.add("Q", dma(dscrB[:], BCsbB[:]),
                      deps=[v_bc] + last_dscr_Q, inc=16, sem="QB1")
        q_bc1c = P.add("Q", dma(dscrC[:], BCf32[NS:32, :]),
                       deps=[v_bc] + last_dscr_Q, inc=16, sem="QB1C")
        last_bcsb_Q = q_bc1c

        ev_last = None
        for ht in range(KH):
            bank = ht % 4
            t_ms = P.add("T", (lambda ht_, bank_: (lambda t: t.matmul(
                PA[bank_][:, 0:T], sb_wdt[:, ht_, :], dtpbf[:],
                start=True, stop=True)))(ht, bank),
                deps=[v_dtp, v_wdt, pa_free[bank]], inc=1)
            deps = [t_ms]
            if ht == 0:
                deps += [last_yevac]
            ev_last = P.add("V", (lambda ht_, bank_: (lambda v: v.tensor_scalar(
                out=ybuf[:, ht_, :], in0=PA[bank_][:, 0:T],
                scalar1=sb_bdt[:, ht_:ht_ + 1], scalar2=None, op0=ALU.add)))(ht, bank),
                deps=deps, inc=1)
            pa_free[bank] = ev_last
        v_dtpre = ev_last
        P.add("V", lambda v: v.tensor_scalar(out=flat(dtb), in0=flat(ybuf),
              scalar1=0.0, scalar2=None, op0=ALU.max),
              deps=[v_dtpre, conv_V, s_uc], inc=1)
        P.add("V", lambda v: v.tensor_scalar(out=flat(S1), in0=flat(ybuf),
              scalar1=0.0, scalar2=None, op0=ALU.min),
              deps=[v_dtpre, s1_free], inc=1)
        v_mn = P.snap("V")
        P.add("S", lambda s: s.activation(out=flat(S2), in_=flat(S1),
              func=AF.Exp), deps=[v_mn, s2_free[0], s2_free[1]], inc=1)
        P.add("S", lambda s: s.activation(out=flat(S1), in_=flat(S2),
              func=AF.Ln, bias=1.0), inc=1)
        s_spl = P.snap("S")
        s2_free = [s_spl, s2_free[1]]
        v_dt = P.add("V", lambda v: v.tensor_tensor(out=flat(dtb), in0=flat(dtb),
              in1=flat(S1), op=ALU.add), deps=[s_spl], inc=1)

        last_dt_readers = []
        last_dscr_Q = []
        t_imm_last = None
        v_scan_prev = ("V", 0)
        da_free = [("V", 0), ("V", 0)]
        for n in range(NS):
            par = n % 2
            q_bb = P.add("Q", (lambda n_, par_: (lambda q: q.dma_start(
                out=Bb[par_][:],
                in_=bass.AP(dscrB.tensor, n_ * T, [[0, 128], [1, T]]))))(n, par),
                deps=[q_bc1, bb_free[par]], inc=16, sem=f"QBB{par}")
            q_cc = P.add("Q", (lambda n_, par_: (lambda q: q.dma_start(
                out=Cb[par_][:],
                in_=bass.AP(dscrC.tensor, n_ * T, [[0, 128], [1, T]]))))(n, par),
                deps=[q_bc1c, cb_free[par]], inc=16, sem=f"QCC{par}")
            last_dscr_Q += [q_bb, q_cc]
            dAbuf = S1
            s_da = P.add("S", (lambda n_, b_: (lambda s: s.activation(
                out=flat(b_), in_=flat(dtb), func=AF.Exp,
                scale=sb_Arep[:, n_:n_ + 1])))(n, dAbuf),
                deps=[v_dt, v_arep, v_scan_prev], inc=1)
            last_dt_readers.append(s_da)
            v_r = P.add("V", (lambda par_: (lambda v: v.tensor_tensor(
                out=S2[:], in0=uc[:], in1=bap(Bb[par_][:], KH, T),
                op=ALU.mult)))(par),
                deps=[q_bb, s_uc, s2_free[0], s2_free[1]], inc=1)
            bb_free[par] = v_r
            s2_free = [v_r, s2_free[1]]

            def mk_scan(n_, b_):
                def f(v):
                    ins = None
                    for k in range(KH):
                        ins = v.tensor_tensor_scan(
                            out=S3[:, k, :], data0=b_[:, k, :], data1=S2[:, k, :],
                            initial=hlast[:, n_, k:k + 1],
                            op0=ALU.mult, op1=ALU.add)
                    return ins
                return f
            v_scan = P.add("V", mk_scan(n, dAbuf), deps=[s_da, s3_free], inc=1)
            v_scan_prev = v_scan
            da_free[n % 2] = v_scan
            P.add("V", (lambda n_: (lambda v: v.tensor_copy(
                out=hlast[:, n_, :],
                in_=S3[:, :, T - 1:T].rearrange("p a o -> p (a o)"))))(n), inc=1)
            v_term = P.add("V", (lambda par_: (lambda v: v.tensor_tensor(
                out=TMv, in0=S3[:], in1=bap(Cb[par_][:], KH, T),
                op=ALU.mult)))(par),
                deps=[q_cc, last_uext_T, conv_V], inc=1)
            cb_free[par] = v_term
            s3_free = v_term

            def mk_imm(n_):
                def f(t):
                    ins = None
                    for k in range(8):
                        ins = t.matmul(
                            PBANKS[k][:, 0:512], ident[:],
                            bass.AP(TMv.tensor, 2 * k * E,
                                    [[KH * E, 128], [E, 2], [1, T]]),
                            start=(n_ == 0), stop=(n_ == NS - 1),
                            skip_group_check=True)
                    return ins
                return f
            imm_deps = [v_term, g_ident]
            if n == 0:
                imm_deps += list(pa_free) + [pb_free, pc_free, pd_free, pe_free]
            t_imm = P.add("T", mk_imm(n), deps=imm_deps, inc=1)
            last_uext_T = t_imm
            t_imm_last = t_imm
        for k in range(8):
            ev = P.add("V", (lambda k_: (lambda v: v.tensor_copy(
                out=ybuf[:, 2 * k_:2 * k_ + 2, :].rearrange("p a t -> p (a t)"),
                in_=PBANKS[k_][:, 0:512])))(k),
                deps=[t_imm_last, last_dt_readers[-1]], inc=1)
            if k < 4:
                pa_free[k] = ev
        last_yevac = P.snap("V")
        pb_free = pc_free = pd_free = pe_free = last_yevac
        last_uext_V = last_yevac

        v_yd = None
        for ct in range(KH):
            v_yd = P.add("V", (lambda ct_: (lambda v: v.scalar_tensor_tensor(
                out=ybuf[:, ct_, :], in0=uc[:, ct_, :],
                scalar=sb_D[:, ct_:ct_ + 1], in1=ybuf[:, ct_, :],
                op0=ALU.mult, op1=ALU.add)))(ct), deps=[last_yevac, s_uc], inc=1)
        P.add("V", lambda v: v.tensor_tensor(
            out=flat(ybuf), in0=flat(ybuf), in1=flat(sz), op=ALU.mult),
            deps=[v_yd, s_zevac[-1]], inc=1)
        v_yfin = P.add("V", lambda v: v.tensor_copy(out=ucbf[:], in_=ybuf[:]),
                       deps=[last_ucbf_G, last_ucbf_T, t_xdbl], inc=1)
        v_ylo = P.add("V", lambda v: v.scalar_tensor_tensor(
            out=flat(S2), in0=flat(ucbf), scalar=-1.0, in1=flat(ybuf),
            op0=ALU.mult, op1=ALU.add), deps=[s2_free[0], s2_free[1]], inc=1)
        s2_free = [v_ylo, s2_free[1]]
        last_sz_reader = v_yfin

        q_wout = {}
        for g in range(2):
            q_wout[g] = P.add("Q", (lambda g_: (lambda q: q.dma_start(
                out=ring_out[:, g_ % 2], in_=wsout[:, g_])))(g),
                deps=[ring_out_free[g % 2]], inc=16, sem=f"QO{g % 2}")
        v_pre1 = []
        t_wout_last = None
        for a in range(KD):
            bank = a % 4
            slot = a % 2

            def mk_wout(a_, bank_, slot_):
                def f(t):
                    ins = None
                    for kt in range(KH):
                        ins = t.matmul(PA[bank_][:, 0:T], ring_out[:, slot_, 0, kt, :],
                                       ucbf[:, kt, :], start=(kt == 0), stop=False)
                    for kt in range(KH):
                        ins = t.matmul(PA[bank_][:, 0:T], ring_out[:, slot_, 0, kt, :],
                                       S2[:, kt, :], start=False, stop=False)
                    for kt in range(KH):
                        ins = t.matmul(PA[bank_][:, 0:T], ring_out[:, slot_, 1, kt, :],
                                       ucbf[:, kt, :], start=False,
                                       stop=(kt == KH - 1))
                    return ins
                return f
            t_ms = P.add("T", mk_wout(a, bank, slot),
                         deps=[v_yfin, v_ylo, q_wout[a], pa_free[bank]], inc=1)
            ring_out_free[slot] = t_ms
            t_wout_last = t_ms
            if a + 2 < KD:
                q_wout[a + 2] = P.add("Q", (lambda g_: (lambda q: q.dma_start(
                    out=ring_out[:, g_ % 2], in_=wsout[:, g_])))(a + 2),
                    deps=[t_ms], inc=16, sem=f"QO{a % 2}")
            ev = P.add("V", (lambda a_, bank_: (lambda v: v.scalar_tensor_tensor(
                out=ybuf[:, a_, :], in0=x_ext[:, a_, 1:T + 1], scalar=2.0,
                in1=PA[bank_][:, 0:T], op0=ALU.mult, op1=ALU.add)))(a, bank),
                deps=[t_ms, q_x], inc=1)
            pa_free[bank] = ev
            v_pre1.append(ev)
        last_x_reader = v_pre1[-1]
        last_ucbf_T = t_wout_last
        s2_free = [s2_free[0], t_wout_last]

        if j > 0:
            P.add("V", lambda v: v.tensor_copy(out=x1[:, :, 0:1], in_=xbcol[:]),
                  deps=[v_pre1[-1]] + last_x1_readers, inc=1)
        v_x1c0 = P.snap("V")
        ln1_outs, pb_free, pc_free, pd_free, pe_free = emit_ln(
            lambda a_: ybuf[:, a_, :],
            [v_pre1[-1]] + last_x1_readers,
            lambda a_: x1[:, a_, 1:T + 1], pb_free, pc_free, pd_free, pe_free,
            v_scan_prev)
        v_x1 = ln1_outs[-1]

        mix_ms = []
        for i_rk, (mu, omu, off) in enumerate(
                ((sb_mur, sb_omur, 0), (sb_muk, sb_omuk, 8))):
            for a in range(KD):
                def mk_mix(a_, mu_, omu_, off_):
                    def f(e):
                        e.tensor_scalar(out=S1[:, 0, :],
                                        in0=x1[:, a_, 1:T + 1],
                                        scalar1=mu_[:, a_:a_ + 1], scalar2=None,
                                        op0=ALU.mult)
                        return e.scalar_tensor_tensor(
                            out=ucbf[:, off_ + a_, :], in0=x1[:, a_, 0:T],
                            scalar=omu_[:, a_:a_ + 1], in1=S1[:, 0, :],
                            op0=ALU.mult, op1=ALU.add)
                    return f
                mix_ms.append(P.add("V", mk_mix(a, mu, omu, off),
                              deps=[v_x1, v_x1c0, v_yfin, t_wout_last], inc=1))
        v_xb = P.add("V", lambda v: v.tensor_copy(out=xbcol[:], in_=x1[:, :, T:T + 1]),
                     deps=[v_x1], inc=1)

        def ffn_dma(ws, base, g, deps):
            return P.add("Q", (lambda g_, ws_, b_: (lambda q: q.dma_start(
                out=ring_ffn[:, (b_ + g_) % 2, 0:KD, :], in_=ws_[:, g_])))(
                g, ws, base), deps=deps, inc=16, sem=f"QF{(base + g) % 2}")

        q_wr = {g: ffn_dma(wsr, 0, g, [ring_ffn_free[g % 2]]) for g in range(2)}
        q_wk = {}
        q_wv = {}
        rr_ms = []
        for a in range(KD):
            bank = a % 4
            t_ms = P.add("T", (lambda a_, bank_: (lambda t: [
                t.matmul(PA[bank_][:, 0:T], ring_ffn[:, a_ % 2, kt, :],
                         ucbf[:, kt, :], start=(kt == 0), stop=(kt == KD - 1))
                for kt in range(KD)][-1]))(a, bank),
                deps=mix_ms + [q_wr[a], pa_free[bank]], inc=1)
            ring_ffn_free[a % 2] = t_ms
            last_ucbf_T = t_ms
            if a + 2 < KD:
                q_wr[a + 2] = ffn_dma(wsr, 0, a + 2, [t_ms])
            else:
                q_wk[a + 2 - KD] = ffn_dma(wsk, KD, a + 2 - KD, [t_ms])
            ev = P.add("S", (lambda a_, bank_: (lambda s: s.activation(
                out=uc[:, a_, :], in_=PA[bank_][:, 0:T], func=AF.Sigmoid)))(a, bank),
                deps=[t_ms, v_yd, s_uc], inc=1)
            pa_free[bank] = ev
            rr_ms.append(ev)
        kact_ms = []
        for a in range(KD):
            bank = a % 4
            t_ms = P.add("T", (lambda a_, bank_: (lambda t: [
                t.matmul(PA[bank_][:, 0:T], ring_ffn[:, (KD + a_) % 2, kt, :],
                         ucbf[:, 8 + kt, :], start=(kt == 0), stop=(kt == KD - 1))
                for kt in range(KD)][-1]))(a, bank),
                deps=mix_ms + [q_wk[a], pa_free[bank]], inc=1)
            ring_ffn_free[a % 2] = t_ms
            last_ucbf_T = t_ms
            if a + 2 < KD:
                q_wk[a + 2] = ffn_dma(wsk, KD, a + 2, [t_ms])
            else:
                q_wv[a + 2 - KD] = ffn_dma(wsv, 2 * KD, a + 2 - KD, [t_ms])
            ev = P.add("V", (lambda a_, bank_: (lambda v: v.tensor_scalar(
                out=uc[:, 8 + a_, :], in0=PA[bank_][:, 0:T],
                scalar1=0.0, scalar2=None, op0=ALU.max)))(a, bank),
                deps=[t_ms, v_yd], inc=1)
            pa_free[bank] = ev
            kact_ms.append(P.add("S", (lambda a_: (lambda s: s.activation(
                out=S2[:, a_, :], in_=uc[:, 8 + a_, :], func=AF.Square)))(a),
                deps=[ev, s2_free[0], s2_free[1]], inc=1))
        v_pre2 = []
        for a in range(KD):
            bank = a % 4
            t_ms = P.add("T", (lambda a_, bank_: (lambda t: [
                t.matmul(PA[bank_][:, 0:T], ring_ffn[:, a_ % 2, kt, :],
                         S2[:, kt, :], start=(kt == 0), stop=(kt == KD - 1))
                for kt in range(KD)][-1]))(a, bank),
                deps=kact_ms + [q_wv[a], pa_free[bank]], inc=1)
            ring_ffn_free[a % 2] = t_ms
            ffn_v_last = t_ms
            if a + 2 < KD:
                q_wv[a + 2] = ffn_dma(wsv, 2 * KD, a + 2, [t_ms])
            ev = P.add("V", (lambda a_, bank_: (lambda v: v.tensor_tensor(
                out=S3[:, a_, :], in0=PA[bank_][:, 0:T],
                in1=uc[:, a_, :], op=ALU.mult)))(a, bank),
                deps=[t_ms, rr_ms[a], s3_free] + mix_ms, inc=1)
            pa_free[bank] = ev
            v_pre2.append(P.add("G", (lambda a_: (lambda g: g.tensor_tensor(
                out=ybuf[:, 8 + a_, :], in0=S3[:, a_, :],
                in1=x1[:, a_, 1:T + 1], op=ALU.add)))(a), deps=[ev], inc=1))
        s3_free = v_pre2[-1]
        s2_free = [s2_free[0], ffn_v_last]
        last_ucbf_G = ("G", 0)
        last_x1_readers = [v_pre2[-1], v_xb, mix_ms[-1]]

        ln2_outs, pb_free, pc_free, pd_free, pe_free = emit_ln(
            lambda a_: ybuf[:, 8 + a_, :],
            v_pre2 + [ln1_outs[-1], kact_ms[-1]],
            lambda a_: uc[:, a_, :], pb_free, pc_free, pd_free, pe_free,
            ln1_outs[-1])
        q_out = P.add("Q", (lambda j_: (lambda q: q.dma_start(
            out=outTr[:, :, j_ * T:(j_ + 1) * T],
            in_=uc[:, 0:KD, :])))(j), deps=[ln2_outs[-1]], inc=16, sem="QS")
        last_uc_war = q_out
        last_yevac = ln2_outs[-1]
        s1_free = ln2_outs[-1]

    totals = dict(P.cnt)

    block_cm = nc.Block()
    block = block_cm.__enter__()

    @block.sync
    def _(sync):
        P.emit("Q", sync, sems)
        for name, val in totals.items():
            if val > 0:
                sync.wait_ge(sems[name], val)

    @block.vector
    def _(v):
        P.emit("V", v, sems)

    @block.scalar
    def _(s):
        P.emit("S", s, sems)

    @block.gpsimd
    def _(g):
        P.emit("G", g, sems)

    @block.tensor
    def _(t):
        P.emit("T", t, sems)

    block_cm.__exit__(None, None, None)
    return nc


_NC_CACHE = None


def kernel(**inputs):
    global _NC_CACHE
    x = np.asarray(inputs["L0_attn_x"])
    B = x.shape[0]
    if _NC_CACHE is None:
        _NC_CACHE = build_nc()
    nc = _NC_CACHE

    shared = {
        "w_inT": np.ascontiguousarray(np.asarray(inputs["L0_attn_W_in"]).T),
        "convw": np.ascontiguousarray(np.asarray(inputs["L0_attn_conv_w"])[:, 0, :]),
        "convb": np.ascontiguousarray(np.asarray(inputs["L0_attn_conv_b"])),
        "w_xprojT": np.ascontiguousarray(np.asarray(inputs["L0_attn_W_xproj"]).T),
        "w_dtT": np.ascontiguousarray(np.asarray(inputs["L0_attn_W_dt"]).T),
        "b_dt": np.ascontiguousarray(np.asarray(inputs["L0_attn_b_dt"])),
        "A_log": np.ascontiguousarray(np.asarray(inputs["L0_attn_A_log"])),
        "Dvec": np.ascontiguousarray(np.asarray(inputs["L0_attn_D"])),
        "w_outT": np.ascontiguousarray(np.asarray(inputs["L0_attn_W_out"]).T),
        "w_rT": np.ascontiguousarray(np.asarray(inputs["L0_ffn_W_r"]).T),
        "w_kT": np.ascontiguousarray(np.asarray(inputs["L0_ffn_W_k"]).T),
        "w_vT": np.ascontiguousarray(np.asarray(inputs["L0_ffn_W_v"]).T),
        "mu_r": np.ascontiguousarray(np.asarray(inputs["L0_ffn_mu_r"])),
        "mu_k": np.ascontiguousarray(np.asarray(inputs["L0_ffn_mu_k"])),
    }
    in_maps = []
    for b in range(B):
        m = dict(shared)
        m["xT"] = np.ascontiguousarray(x[b].T)
        in_maps.append(m)
    res = run_bass_kernel_spmd(nc, in_maps, core_ids=list(range(B)))
    out = np.stack([np.asarray(res.results[b]["outT"]).T for b in range(B)], axis=0)
    return out.astype(np.float32)


# version 12
# speedup vs baseline: 1.0805x; 1.0355x over previous
import numpy as np
import concourse.bass as bass
import concourse.mybir as mybir
from concourse.bass_utils import run_bass_kernel_spmd

F32 = mybir.dt.float32
BF16 = mybir.dt.bfloat16
AF = mybir.ActivationFunctionType
ALU = mybir.AluOpType

L, DM, H, NS, RK = 4096, 1024, 2048, 16, 64
T = 256
NCH = L // T
E = T + 2
KD = DM // 128
KH = H // 128
NE = 2 * H // 128

ENGS = ("T", "V", "S", "G", "Q")


def bap(ap, reps, inner):
    return bass.AP(ap.tensor, ap.offset, [list(ap.ap[0]), [0, reps], [1, inner]])


def flat(handle_or_ap):
    ap = handle_or_ap.ap() if hasattr(handle_or_ap, "ap") and not isinstance(
        handle_or_ap, bass.AP) else handle_or_ap
    return ap.rearrange("p a t -> p (a t)")


class Prog:
    def __init__(self):
        self.ops = {e: [] for e in ENGS}
        self.cnt = {}
        self.hw = {}

    def snap(self, sem):
        return (sem, self.cnt.get(sem, 0))

    def add(self, eng, fn, deps=(), inc=0, sem=None):
        sem = sem or eng
        self.ops[eng].append((fn, tuple(deps), inc, sem))
        if inc:
            self.cnt[sem] = self.cnt.get(sem, 0) + inc
        return self.snap(sem)

    def compute_needed(self):
        needed = {}
        for eng in ENGS:
            for fn, deps, inc, sem in self.ops[eng]:
                for (de, dv) in deps:
                    if de != eng or eng == "Q":
                        needed.setdefault(de, set()).add(dv)
        for sem, tot in self.cnt.items():
            needed.setdefault(sem, set()).add(tot)
        self.needed = needed

    def emit(self, eng, handle, sems):
        if not hasattr(self, "needed"):
            self.compute_needed()
        run = {}
        pend = {}
        for fn, deps, inc, sem in self.ops[eng]:
            for (de, dv) in deps:
                if dv <= 0:
                    continue
                if de == eng and eng != "Q":
                    continue
                key = (eng, de)
                if self.hw.get(key, -1) >= dv:
                    continue
                self.hw[key] = dv
                handle.wait_ge(sems[de], dv)
            instr = fn(handle)
            if inc:
                assert instr is not None, "milestone op returned no instruction"
                run[sem] = run.get(sem, 0) + inc
                pend[sem] = pend.get(sem, 0) + inc
                if eng in ("V", "S", "G"):
                    if run[sem] in self.needed.get(sem, ()):
                        handle.drain(fusable=True).then_inc(sems[sem], pend[sem])
                        pend[sem] = 0
                else:
                    instr.then_inc(sems[sem], inc)
        for sem, p in pend.items():
            if p:
                handle.drain(fusable=True).then_inc(sems[sem], p)


def build_nc():
    nc = bass.Bass("TRN2", target_bir_lowering=False)

    xT = nc.dram_tensor("xT", [DM, L], F32, kind="ExternalInput").ap()
    w_inT = nc.dram_tensor("w_inT", [DM, 2 * H], F32, kind="ExternalInput").ap()
    convw = nc.dram_tensor("convw", [H, 3], F32, kind="ExternalInput").ap()
    convb = nc.dram_tensor("convb", [H], F32, kind="ExternalInput").ap()
    w_xprojT = nc.dram_tensor("w_xprojT", [H, 96], F32, kind="ExternalInput").ap()
    w_dtT = nc.dram_tensor("w_dtT", [RK, H], F32, kind="ExternalInput").ap()
    b_dt = nc.dram_tensor("b_dt", [H], F32, kind="ExternalInput").ap()
    A_log = nc.dram_tensor("A_log", [NS], F32, kind="ExternalInput").ap()
    Dvec = nc.dram_tensor("Dvec", [H], F32, kind="ExternalInput").ap()
    w_outT = nc.dram_tensor("w_outT", [H, DM], F32, kind="ExternalInput").ap()
    w_rT = nc.dram_tensor("w_rT", [DM, DM], F32, kind="ExternalInput").ap()
    w_kT = nc.dram_tensor("w_kT", [DM, DM], F32, kind="ExternalInput").ap()
    w_vT = nc.dram_tensor("w_vT", [DM, DM], F32, kind="ExternalInput").ap()
    mu_r = nc.dram_tensor("mu_r", [DM], F32, kind="ExternalInput").ap()
    mu_k = nc.dram_tensor("mu_k", [DM], F32, kind="ExternalInput").ap()
    outT = nc.dram_tensor("outT", [DM, L], F32, kind="ExternalOutput").ap()

    wsin = nc.dram_tensor("wsin", [128, NE, 2, KD, 128], BF16, kind="Internal").ap()
    wsout = nc.dram_tensor("wsout", [128, KD, 2, KH, 128], BF16, kind="Internal").ap()
    wsr = nc.dram_tensor("wsr", [128, KD, KD, 128], BF16, kind="Internal").ap()
    wsk = nc.dram_tensor("wsk", [128, KD, KD, 128], BF16, kind="Internal").ap()
    wsv = nc.dram_tensor("wsv", [128, KD, KD, 128], BF16, kind="Internal").ap()
    dscrB = nc.dram_tensor("dscrB", [NS, T], BF16, kind="Internal").ap()
    dscrC = nc.dram_tensor("dscrC", [NS, T], F32, kind="Internal").ap()

    A = nc.alloc_sbuf_tensor
    sb_wxp = A("sb_wxp", [128, KH, 96], F32)
    sb_wdt = A("sb_wdt", [RK, KH, 128], BF16)
    sb_convw = A("sb_convw", [128, KH, 3], F32)
    sb_convb = A("sb_convb", [128, KH], F32)
    sb_bdt = A("sb_bdt", [128, KH], F32)
    sb_D = A("sb_D", [128, KH], F32)
    sb_mur = A("sb_mur", [128, KD], F32)
    sb_muk = A("sb_muk", [128, KD], F32)
    sb_omur = A("sb_omur", [128, KD], F32)
    sb_omuk = A("sb_omuk", [128, KD], F32)
    sb_Arep = A("sb_Arep", [128, NS], F32)
    sb_Alog = A("sb_Alog", [1, NS], F32)
    sb_Aflat = A("sb_Aflat", [1, NS], F32)
    ones_r = A("ones_r", [1, 128], F32)
    ones_c = A("ones_c", [128, 1], F32)
    ident = A("ident", [128, 128], F32)
    ring_in = A("ring_in", [128, 2, 2, KD, 128], BF16)
    ring_out = A("ring_out", [128, 2, 2, KH, 128], BF16)
    ring_ffn = A("ring_ffn", [128, 2, KD, 128], BF16)
    x_ext = A("x_ext", [128, KD, E], F32)
    xbf = A("xbf", [128, KD, E], BF16)
    xlo = A("xlo", [128, KD, E], BF16)
    u_ext = A("u_ext", [128, KH, E], F32)
    sz = A("sz", [128, KH, T], F32)
    uc = A("uc", [128, KH, T], F32)
    ucbf = A("ucbf", [128, KH, T], BF16)
    dtb = A("dtb", [128, KH, T], F32)
    ybuf = A("ybuf", [128, KH, T], F32)
    S1 = A("S1", [128, KH, T], F32)
    S2 = A("S2", [128, KH, T], BF16)
    S3 = A("S3", [128, KH, T], F32)
    Bb = [A("Bba", [128, T], BF16), A("Bbb", [128, T], BF16)]
    Cb = [A("Cba", [128, T], F32), A("Cbb", [128, T], F32)]
    hlast = A("hlast", [128, NS, KH], F32)
    dtpbf = A("dtpbf", [RK, T], BF16)
    BCf32 = A("BCf32", [32, T], F32)
    BCsbB = A("BCsbB", [NS, T], BF16)
    x1 = A("x1", [128, KD, 1 + T], F32)
    xbcol = A("xbcol", [128, KD, 1], F32)
    rowm = A("rowm", [1, T], F32)
    rowv = A("rowv", [1, T], F32)
    rowt = A("rowt", [1, T], F32)
    TMv = u_ext.ap()[:, :, 0:T]
    wtmp = ybuf.ap()[:, :, 0:128]
    wtmpb = ucbf.ap()[:, :, 0:128]

    PA = [nc.alloc_psum_tensor(f"PA{i}", [128, 512], F32) for i in range(4)]
    PB = nc.alloc_psum_tensor("PB", [128, 512], F32)
    PC = nc.alloc_psum_tensor("PC", [128, 512], F32)
    PD = nc.alloc_psum_tensor("PD", [128, 512], F32)
    PE2 = nc.alloc_psum_tensor("PE2", [128, 512], F32)
    PBANKS = PA + [PB, PC, PD, PE2]

    qsems = (["QL", "QP", "QQ", "QX", "QS", "QB1"]
             + [f"QI{i}" for i in range(2)] + [f"QO{i}" for i in range(2)]
             + [f"QF{i}" for i in range(2)]
             + ["QB1C", "QBB0", "QBB1", "QCC0", "QCC1"])
    sems = {e: nc.alloc_semaphore("sem" + e) for e in list(ENGS) + qsems}
    P = Prog()

    def dma(dst, src, slow=False):
        if slow:
            return lambda q: q.dma_start(out=dst, in_=src,
                                         allow_slow_non_contiguous=True)
        return lambda q: q.dma_start(out=dst, in_=src)

    for dst, src in (
        (sb_convw[:], convw.rearrange("(k p) c -> p k c", p=128)),
        (sb_convb[:], convb.rearrange("(k p) -> p k", p=128)),
        (sb_bdt[:], b_dt.rearrange("(k p) -> p k", p=128)),
        (sb_D[:], Dvec.rearrange("(k p) -> p k", p=128)),
        (sb_mur[:], mu_r.rearrange("(k p) -> p k", p=128)),
        (sb_muk[:], mu_k.rearrange("(k p) -> p k", p=128)),
        (sb_Alog[:], bass.AP(A_log.tensor, 0, [[NS, 1], [1, NS]])),
    ):
        P.add("Q", dma(dst, src, slow=True), inc=16, sem="QL")
    q_small = P.snap("QL")

    P.add("V", lambda v: v.memset(ones_r[:], 1.0), inc=1)
    P.add("V", lambda v: v.memset(ones_c[:], 1.0), inc=1)
    P.add("V", lambda v: v.memset(hlast[:], 0.0), inc=1)
    P.add("V", lambda v: v.memset(x_ext[:, :, 0:1], 0.0), inc=1)
    P.add("V", lambda v: v.memset(x1[:, :, 0:1], 0.0), inc=1)
    v_init = P.snap("V")
    P.add("G", lambda g: g.memset(ident[:], 0.0), inc=1)
    P.add("G", lambda g: g.affine_select(
        out=ident[:], in_=ident[:], compare_op=ALU.not_equal, fill=1.0,
        base=0, pattern=[[-1, 128]], channel_multiplier=1), inc=1)
    g_ident = P.snap("G")

    s_ae = P.add("S", lambda s: s.activation(out=sb_Aflat[:], in_=sb_Alog[:],
                 func=AF.Exp), deps=[q_small], inc=1)
    v_aneg = P.add("V", lambda v: v.tensor_scalar(
        out=sb_Aflat[:], in0=sb_Aflat[:], scalar1=-1.0, scalar2=None,
        op0=ALU.mult), deps=[s_ae], inc=1)
    t_arep = P.add("T", lambda t: t.matmul(PB[0:128, 0:NS], ones_r[:], sb_Aflat[:],
                   start=True, stop=True), deps=[v_aneg, v_init], inc=1)
    v_arep = P.add("V", lambda v: v.tensor_copy(out=sb_Arep[:], in_=PB[0:128, 0:NS]),
                   deps=[t_arep], inc=1)
    pb_free = v_arep
    pc_free = ("V", 0)
    pd_free = ("V", 0)
    pe_free = ("V", 0)

    P.add("V", lambda v: v.tensor_scalar(out=sb_omur[:], in0=sb_mur[:],
          scalar1=-1.0, scalar2=1.0, op0=ALU.mult, op1=ALU.add), deps=[q_small], inc=1)
    P.add("V", lambda v: v.tensor_scalar(out=sb_omuk[:], in0=sb_muk[:],
          scalar1=-1.0, scalar2=1.0, op0=ALU.mult, op1=ALU.add), deps=[q_small], inc=1)

    q_wxp = P.add("Q", dma(sb_wxp[:], w_xprojT.rearrange("(k p) e -> p k e", p=128)),
                  inc=16, sem="QL")
    q_wdt = P.add("Q", dma(wtmp[0:RK, :, :],
                  w_dtT.rearrange("r (k p) -> r k p", k=KH)), inc=16, sem="QL")
    v_wdt = P.add("V", lambda v: v.tensor_copy(out=sb_wdt[:], in_=wtmp[0:RK, :, :]),
                  deps=[q_wdt], inc=1)

    pieces = []
    w_inT_r = w_inT.rearrange("(k p) e -> p k e", p=128)
    for et in range(NE):
        pieces.append((w_inT_r[:, :, et * 128:(et + 1) * 128], KD, wsin[:, et], True))
    w_outT_r = w_outT.rearrange("(k p) e -> p k e", p=128)
    for a in range(KD):
        pieces.append((w_outT_r[:, :, a * 128:(a + 1) * 128], KH, wsout[:, a], True))
    for (w, ws) in ((w_rT, wsr), (w_kT, wsk), (w_vT, wsv)):
        w_r = w.rearrange("(k p) e -> p k e", p=128)
        for a in range(KD):
            pieces.append((w_r[:, :, a * 128:(a + 1) * 128], KD, ws[:, a], False))

    prev_v = v_wdt
    prev_store = ("QQ", 0)
    for (src, ntile, dst, lo) in pieces:
        ld = P.add("Q", dma(wtmp[:, 0:ntile, :], src), deps=[prev_v], inc=16, sem="QP")
        if lo:
            P.add("V", (lambda nt: (lambda v: v.tensor_copy(
                out=wtmpb[:, 0:nt, :], in_=wtmp[:, 0:nt, :])))(ntile),
                deps=[ld, prev_store], inc=1)
            h_v = P.snap("V")
            hst = P.add("Q", (lambda d, nt: (lambda q: q.dma_start(
                out=d[:, 0], in_=wtmpb[:, 0:nt, :])))(dst, ntile),
                deps=[h_v], inc=16, sem="QQ")
            P.add("V", (lambda nt: (lambda v: v.tensor_tensor(
                out=wtmpb[:, 0:nt, :], in0=wtmp[:, 0:nt, :],
                in1=wtmpb[:, 0:nt, :], op=ALU.subtract)))(ntile),
                deps=[hst], inc=1)
            prev_v = P.snap("V")
            prev_store = P.add("Q", (lambda d, nt: (lambda q: q.dma_start(
                out=d[:, 1], in_=wtmpb[:, 0:nt, :])))(dst, ntile),
                deps=[prev_v], inc=16, sem="QQ")
        else:
            P.add("V", (lambda nt: (lambda v: v.tensor_copy(
                out=wtmpb[:, 0:nt, :], in_=wtmp[:, 0:nt, :])))(ntile),
                deps=[ld, prev_store], inc=1)
            prev_v = P.snap("V")
            prev_store = P.add("Q", (lambda d, nt: (lambda q: q.dma_start(
                out=d, in_=wtmpb[:, 0:nt, :])))(dst, ntile),
                deps=[prev_v], inc=16, sem="QQ")
    q_wbig = P.snap("QQ")

    last_x_reader = ("V", 0)
    last_uext_V = ("V", 0)
    last_uext_T = ("T", 0)
    last_sz_reader = ("V", 0)
    last_dt_readers = []
    last_yevac = ("V", 0)
    last_x1_readers = []
    last_uc_war = ("QS", 0)
    last_ucbf_T = ("T", 0)
    last_ucbf_G = ("G", 0)
    last_xbf_T = ("T", 0)
    last_bcsb_Q = ("QB1", 0)
    last_dscr_Q = []
    s1_free = ("V", 0)
    s2_free = [("V", 0), ("T", 0)]
    s3_free = ("V", 0)
    bb_free = [("V", 0), ("V", 0)]
    cb_free = [("V", 0), ("V", 0)]
    pa_free = [v_init] * 4
    ring_in_free = [("T", 0)] * 2
    ring_out_free = [("T", 0)] * 2
    ring_ffn_free = [("T", 0)] * 2

    xTr = xT.rearrange("(k p) t -> p k t", p=128)
    outTr = outT.rearrange("(k p) t -> p k t", p=128)

    def emit_ln(src_of, pre_ms, dst_of, pbf, pcf, pdf, pef, s1f):
        t_sx = P.add("T", lambda t: [t.matmul(
            PD[0:1, 0:T], ones_c[:], src_of(a), start=(a == 0),
            stop=(a == KD - 1)) for a in range(KD)][-1],
            deps=pre_ms + [pdf], inc=1)
        sq_ms = []
        for a in range(KD):
            sq_ms.append(P.add("S", (lambda a_: (lambda s: s.activation(
                out=S1[:, a_, :], in_=src_of(a_), func=AF.Square)))(a),
                deps=pre_ms + [s1f], inc=1))
        t_sq = P.add("T", lambda t: [t.matmul(
            PE2[0:1, 0:T], ones_c[:], S1[:, a, :], start=(a == 0),
            stop=(a == KD - 1)) for a in range(KD)][-1],
            deps=sq_ms + [pef], inc=1)
        P.add("V", lambda v: v.tensor_scalar(out=rowm[:], in0=PD[0:1, 0:T],
              scalar1=1.0 / DM, scalar2=None, op0=ALU.mult), deps=[t_sx], inc=1)
        npdf = P.snap("V")
        P.add("V", lambda v: v.tensor_scalar(out=rowv[:], in0=PE2[0:1, 0:T],
              scalar1=1.0 / DM, scalar2=1e-5, op0=ALU.mult, op1=ALU.add),
              deps=[t_sq], inc=1)
        npef = P.snap("V")
        P.add("V", lambda v: v.tensor_tensor(out=rowt[:], in0=rowm[:],
              in1=rowm[:], op=ALU.mult), inc=1)
        v_var = P.add("V", lambda v: v.tensor_tensor(out=rowv[:], in0=rowv[:],
              in1=rowt[:], op=ALU.subtract), inc=1)
        P.add("S", lambda s: s.activation(out=rowt[:], in_=rowv[:],
              func=AF.Ln), deps=[v_var], inc=1)
        s_rstd = P.add("S", lambda s: s.activation(out=rowt[:], in_=rowt[:],
              func=AF.Exp, scale=-0.5), inc=1)
        t_mrep = P.add("T", lambda t: t.matmul(PB[:, 0:T], ones_r[:], rowm[:],
              start=True, stop=True), deps=[v_var, pbf], inc=1)
        t_rrep = P.add("T", lambda t: t.matmul(PC[:, 0:T], ones_r[:], rowt[:],
              start=True, stop=True), deps=[s_rstd, pcf], inc=1)
        outs = []
        npbf = None
        for a in range(KD):
            P.add("V", (lambda a_: (lambda v: v.tensor_tensor(
                out=S1[:, a_, :], in0=src_of(a_), in1=PB[:, 0:T],
                op=ALU.subtract)))(a), deps=[t_mrep], inc=1)
            if a == KD - 1:
                npbf = P.snap("V")
            outs.append(P.add("V", (lambda a_: (lambda v: v.tensor_tensor(
                out=dst_of(a_), in0=S1[:, a_, :], in1=PC[:, 0:T],
                op=ALU.mult)))(a), deps=[t_rrep], inc=1))
        return outs, npbf, outs[-1], npdf, npef

    for j in range(NCH):
        s0 = j * T

        if j == 0:
            q_x = P.add("Q", dma(x_ext[:, :, 1:E], xTr[:, :, 0:T + 1]),
                        deps=[last_x_reader, q_wbig], inc=16, sem="QX")
        elif j == NCH - 1:
            q_x = P.add("Q", dma(x_ext[:, :, 0:E - 1], xTr[:, :, s0 - 1:s0 + T]),
                        deps=[last_x_reader], inc=16, sem="QX")
        else:
            q_x = P.add("Q", dma(x_ext[:, :, 0:E], xTr[:, :, s0 - 1:s0 + T + 1]),
                        deps=[last_x_reader], inc=16, sem="QX")
        if j == NCH - 1:
            P.add("V", lambda v: v.memset(x_ext[:, :, E - 1:E], 0.0),
                  deps=[last_x_reader], inc=1)
        P.add("V", lambda v: v.tensor_copy(out=xbf[:], in_=x_ext[:]),
              deps=[q_x, last_xbf_T], inc=1)
        v_xbf = P.add("V", lambda v: v.scalar_tensor_tensor(
            out=xlo[:], in0=xbf[:], scalar=-1.0, in1=x_ext[:],
            op0=ALU.mult, op1=ALU.add), deps=[q_x, last_xbf_T], inc=1)

        q_wing = {}
        for g in range(2):
            q_wing[g] = P.add("Q", (lambda g_: (lambda q: q.dma_start(
                out=ring_in[:, g_ % 2], in_=wsin[:, g_])))(g),
                deps=[ring_in_free[g % 2]], inc=16, sem=f"QI{g % 2}")
        v_uevac = []
        s_zevac = []
        t_xz_last = None
        for et in range(NE):
            bank = et % 4
            slot = et % 2

            def mk_xz(et_, bank_, slot_):
                def f(t):
                    ins = None
                    for dk in range(KD):
                        ins = t.matmul(PA[bank_][:, 0:E],
                                       ring_in[:, slot_, 0, dk, :],
                                       xbf[:, dk, :], start=(dk == 0), stop=False)
                    for dk in range(KD):
                        ins = t.matmul(PA[bank_][:, 0:E],
                                       ring_in[:, slot_, 0, dk, :],
                                       xlo[:, dk, :], start=False, stop=False)
                    for dk in range(KD):
                        ins = t.matmul(PA[bank_][:, 0:E],
                                       ring_in[:, slot_, 1, dk, :],
                                       xbf[:, dk, :], start=False,
                                       stop=(dk == KD - 1))
                    return ins
                return f
            t_ms = P.add("T", mk_xz(et, bank, slot),
                         deps=[v_xbf, q_wing[et], pa_free[bank]], inc=1)
            ring_in_free[slot] = t_ms
            t_xz_last = t_ms
            if et + 2 < NE:
                q_wing[et + 2] = P.add("Q", (lambda g_: (lambda q: q.dma_start(
                    out=ring_in[:, g_ % 2], in_=wsin[:, g_])))(et + 2),
                    deps=[t_ms], inc=16, sem=f"QI{(et + 2) % 2}")
            if et < KH:
                deps = [t_ms]
                if et == 0:
                    deps += [last_uext_V, last_uext_T]
                    if j == 0:
                        deps += [q_wbig]
                ev = P.add("V", (lambda et_, bank_: (lambda v: v.tensor_copy(
                    out=u_ext[:, et_, :], in_=PA[bank_][:, 0:E])))(et, bank),
                    deps=deps, inc=1)
                v_uevac.append(ev)
                pa_free[bank] = ev
            else:
                deps = [t_ms]
                if et == KH:
                    deps += [last_sz_reader]
                ev = P.add("S", (lambda et_, bank_: (lambda s: s.activation(
                    out=sz[:, et_ - KH, :], in_=PA[bank_][:, 1:T + 1],
                    func=AF.Silu)))(et, bank), deps=deps, inc=1)
                s_zevac.append(ev)
                pa_free[bank] = ev
        last_xbf_T = t_xz_last

        conv_V = None
        for ct in range(KH):
            deps = [v_uevac[ct]]
            if ct == 0:
                deps = deps + last_dt_readers + [last_uc_war]

            def mk_conv(ct_):
                def f(e):
                    e.tensor_scalar(out=dtb[:, ct_, :], in0=u_ext[:, ct_, 0:T],
                                    scalar1=sb_convw[:, ct_, 0:1],
                                    scalar2=sb_convb[:, ct_:ct_ + 1],
                                    op0=ALU.mult, op1=ALU.add)
                    e.scalar_tensor_tensor(out=dtb[:, ct_, :],
                                           in0=u_ext[:, ct_, 1:T + 1],
                                           scalar=sb_convw[:, ct_, 1:2],
                                           in1=dtb[:, ct_, :],
                                           op0=ALU.mult, op1=ALU.add)
                    return e.scalar_tensor_tensor(
                        out=dtb[:, ct_, :], in0=u_ext[:, ct_, 2:T + 2],
                        scalar=sb_convw[:, ct_, 2:3], in1=dtb[:, ct_, :],
                        op0=ALU.mult, op1=ALU.add)
                return f
            conv_V = P.add("V", mk_conv(ct), deps=deps, inc=1)
        s_uc = P.add("S", lambda s: s.activation(out=flat(uc), in_=flat(dtb),
                     func=AF.Silu), deps=[conv_V, last_uc_war], inc=1)
        v_ucbf = P.add("V", lambda v: v.tensor_copy(out=ucbf[:], in_=uc[:]),
                       deps=[s_uc, last_ucbf_T, last_ucbf_G] +
                       ([q_wbig] if j == 0 else []), inc=1)

        def mk_xdbl(t):
            ins = None
            for kt in range(KH):
                ins = t.matmul(PB[0:96, 0:T], sb_wxp[:, kt, :], uc[:, kt, :],
                               start=(kt == 0), stop=(kt == KH - 1))
            return ins
        t_xdbl = P.add("T", mk_xdbl, deps=[s_uc, q_wxp, pb_free], inc=1)
        v_dtp = P.add("V", lambda v: v.tensor_copy(out=dtpbf[:], in_=PB[0:RK, 0:T]),
                      deps=[t_xdbl], inc=1)
        P.add("V", lambda v: v.tensor_copy(out=BCf32[:], in_=PB[64:96, 0:T]),
              deps=[t_xdbl, last_bcsb_Q], inc=1)
        v_bc = P.add("V", lambda v: v.tensor_copy(out=BCsbB[:], in_=BCf32[0:NS, :]),
                     deps=[], inc=1)
        pb_free = v_bc
        q_bc1 = P.add("Q", dma(dscrB[:], BCsbB[:]),
                      deps=[v_bc] + last_dscr_Q, inc=16, sem="QB1")
        q_bc1c = P.add("Q", dma(dscrC[:], BCf32[NS:32, :]),
                       deps=[v_bc] + last_dscr_Q, inc=16, sem="QB1C")
        last_bcsb_Q = q_bc1c

        ev_last = None
        for ht in range(KH):
            bank = ht % 4
            t_ms = P.add("T", (lambda ht_, bank_: (lambda t: t.matmul(
                PA[bank_][:, 0:T], sb_wdt[:, ht_, :], dtpbf[:],
                start=True, stop=True)))(ht, bank),
                deps=[v_dtp, v_wdt, pa_free[bank]], inc=1)
            deps = [t_ms]
            if ht == 0:
                deps += [last_yevac]
            ev_last = P.add("V", (lambda ht_, bank_: (lambda v: v.tensor_scalar(
                out=ybuf[:, ht_, :], in0=PA[bank_][:, 0:T],
                scalar1=sb_bdt[:, ht_:ht_ + 1], scalar2=None, op0=ALU.add)))(ht, bank),
                deps=deps, inc=1)
            pa_free[bank] = ev_last
        v_dtpre = ev_last
        P.add("V", lambda v: v.tensor_scalar(out=flat(dtb), in0=flat(ybuf),
              scalar1=0.0, scalar2=None, op0=ALU.max),
              deps=[v_dtpre, conv_V, s_uc], inc=1)
        P.add("V", lambda v: v.tensor_scalar(out=flat(S1), in0=flat(ybuf),
              scalar1=0.0, scalar2=None, op0=ALU.min),
              deps=[v_dtpre, s1_free], inc=1)
        v_mn = P.snap("V")
        P.add("S", lambda s: s.activation(out=flat(S2), in_=flat(S1),
              func=AF.Exp), deps=[v_mn, s2_free[0], s2_free[1]], inc=1)
        P.add("S", lambda s: s.activation(out=flat(S1), in_=flat(S2),
              func=AF.Ln, bias=1.0), inc=1)
        s_spl = P.snap("S")
        s2_free = [s_spl, s2_free[1]]
        v_dt = P.add("V", lambda v: v.tensor_tensor(out=flat(dtb), in0=flat(dtb),
              in1=flat(S1), op=ALU.add), deps=[s_spl], inc=1)

        last_dt_readers = []
        last_dscr_Q = []
        t_imm_last = None
        v_scan_prev = ("V", 0)
        da_free = [("V", 0), ("V", 0)]
        for n in range(NS):
            par = n % 2
            q_bb = P.add("Q", (lambda n_, par_: (lambda q: q.dma_start(
                out=Bb[par_][:],
                in_=bass.AP(dscrB.tensor, n_ * T, [[0, 128], [1, T]]))))(n, par),
                deps=[q_bc1, bb_free[par]], inc=16, sem=f"QBB{par}")
            q_cc = P.add("Q", (lambda n_, par_: (lambda q: q.dma_start(
                out=Cb[par_][:],
                in_=bass.AP(dscrC.tensor, n_ * T, [[0, 128], [1, T]]))))(n, par),
                deps=[q_bc1c, cb_free[par]], inc=16, sem=f"QCC{par}")
            last_dscr_Q += [q_bb, q_cc]
            dAbuf = S1
            s_da = P.add("S", (lambda n_, b_: (lambda s: s.activation(
                out=flat(b_), in_=flat(dtb), func=AF.Exp,
                scale=sb_Arep[:, n_:n_ + 1])))(n, dAbuf),
                deps=[v_dt, v_arep, v_scan_prev], inc=1)
            last_dt_readers.append(s_da)
            v_r = P.add("V", (lambda par_: (lambda v: v.tensor_tensor(
                out=S2[:], in0=ucbf[:], in1=bap(Bb[par_][:], KH, T),
                op=ALU.mult)))(par),
                deps=[q_bb, v_ucbf, s2_free[0], s2_free[1]], inc=1)
            bb_free[par] = v_r
            s2_free = [v_r, s2_free[1]]

            def mk_scan(n_, b_):
                def f(v):
                    ins = None
                    for k in range(KH):
                        ins = v.tensor_tensor_scan(
                            out=S3[:, k, :], data0=b_[:, k, :], data1=S2[:, k, :],
                            initial=hlast[:, n_, k:k + 1],
                            op0=ALU.mult, op1=ALU.add)
                    return ins
                return f
            v_scan = P.add("V", mk_scan(n, dAbuf), deps=[s_da, s3_free], inc=1)
            v_scan_prev = v_scan
            da_free[n % 2] = v_scan
            P.add("V", (lambda n_: (lambda v: v.tensor_copy(
                out=hlast[:, n_, :],
                in_=S3[:, :, T - 1:T].rearrange("p a o -> p (a o)"))))(n), inc=1)
            v_term = P.add("V", (lambda par_: (lambda v: v.tensor_tensor(
                out=TMv, in0=S3[:], in1=bap(Cb[par_][:], KH, T),
                op=ALU.mult)))(par),
                deps=[q_cc, last_uext_T, conv_V], inc=1)
            cb_free[par] = v_term
            s3_free = v_term

            def mk_imm(n_):
                def f(t):
                    ins = None
                    for k in range(8):
                        ins = t.matmul(
                            PBANKS[k][:, 0:512], ident[:],
                            bass.AP(TMv.tensor, 2 * k * E,
                                    [[KH * E, 128], [E, 2], [1, T]]),
                            start=(n_ == 0), stop=(n_ == NS - 1),
                            skip_group_check=True)
                    return ins
                return f
            imm_deps = [v_term, g_ident]
            if n == 0:
                imm_deps += list(pa_free) + [pb_free, pc_free, pd_free, pe_free]
            t_imm = P.add("T", mk_imm(n), deps=imm_deps, inc=1)
            last_uext_T = t_imm
            t_imm_last = t_imm
        for k in range(8):
            ev = P.add("V", (lambda k_: (lambda v: v.tensor_copy(
                out=ybuf[:, 2 * k_:2 * k_ + 2, :].rearrange("p a t -> p (a t)"),
                in_=PBANKS[k_][:, 0:512])))(k),
                deps=[t_imm_last, last_dt_readers[-1]], inc=1)
            if k < 4:
                pa_free[k] = ev
        last_yevac = P.snap("V")
        pb_free = pc_free = pd_free = pe_free = last_yevac
        last_uext_V = last_yevac

        v_yd = None
        for ct in range(KH):
            v_yd = P.add("V", (lambda ct_: (lambda v: v.scalar_tensor_tensor(
                out=ybuf[:, ct_, :], in0=uc[:, ct_, :],
                scalar=sb_D[:, ct_:ct_ + 1], in1=ybuf[:, ct_, :],
                op0=ALU.mult, op1=ALU.add)))(ct), deps=[last_yevac, s_uc], inc=1)
        P.add("V", lambda v: v.tensor_tensor(
            out=flat(ybuf), in0=flat(ybuf), in1=flat(sz), op=ALU.mult),
            deps=[v_yd, s_zevac[-1]], inc=1)
        v_yfin = P.add("V", lambda v: v.tensor_copy(out=ucbf[:], in_=ybuf[:]),
                       deps=[last_ucbf_G, last_ucbf_T, t_xdbl], inc=1)
        v_ylo = P.add("V", lambda v: v.scalar_tensor_tensor(
            out=flat(S2), in0=flat(ucbf), scalar=-1.0, in1=flat(ybuf),
            op0=ALU.mult, op1=ALU.add), deps=[s2_free[0], s2_free[1]], inc=1)
        s2_free = [v_ylo, s2_free[1]]
        last_sz_reader = v_yfin

        q_wout = {}
        for g in range(2):
            q_wout[g] = P.add("Q", (lambda g_: (lambda q: q.dma_start(
                out=ring_out[:, g_ % 2], in_=wsout[:, g_])))(g),
                deps=[ring_out_free[g % 2]], inc=16, sem=f"QO{g % 2}")
        v_pre1 = []
        t_wout_last = None
        for a in range(KD):
            bank = a % 4
            slot = a % 2

            def mk_wout(a_, bank_, slot_):
                def f(t):
                    ins = None
                    for kt in range(KH):
                        ins = t.matmul(PA[bank_][:, 0:T], ring_out[:, slot_, 0, kt, :],
                                       ucbf[:, kt, :], start=(kt == 0), stop=False)
                    for kt in range(KH):
                        ins = t.matmul(PA[bank_][:, 0:T], ring_out[:, slot_, 0, kt, :],
                                       S2[:, kt, :], start=False, stop=False)
                    for kt in range(KH):
                        ins = t.matmul(PA[bank_][:, 0:T], ring_out[:, slot_, 1, kt, :],
                                       ucbf[:, kt, :], start=False,
                                       stop=(kt == KH - 1))
                    return ins
                return f
            t_ms = P.add("T", mk_wout(a, bank, slot),
                         deps=[v_yfin, v_ylo, q_wout[a], pa_free[bank]], inc=1)
            ring_out_free[slot] = t_ms
            t_wout_last = t_ms
            if a + 2 < KD:
                q_wout[a + 2] = P.add("Q", (lambda g_: (lambda q: q.dma_start(
                    out=ring_out[:, g_ % 2], in_=wsout[:, g_])))(a + 2),
                    deps=[t_ms], inc=16, sem=f"QO{a % 2}")
            ev = P.add("V", (lambda a_, bank_: (lambda v: v.scalar_tensor_tensor(
                out=ybuf[:, a_, :], in0=x_ext[:, a_, 1:T + 1], scalar=2.0,
                in1=PA[bank_][:, 0:T], op0=ALU.mult, op1=ALU.add)))(a, bank),
                deps=[t_ms, q_x], inc=1)
            pa_free[bank] = ev
            v_pre1.append(ev)
        last_x_reader = v_pre1[-1]
        last_ucbf_T = t_wout_last
        s2_free = [s2_free[0], t_wout_last]

        if j > 0:
            P.add("V", lambda v: v.tensor_copy(out=x1[:, :, 0:1], in_=xbcol[:]),
                  deps=[v_pre1[-1]] + last_x1_readers, inc=1)
        v_x1c0 = P.snap("V")
        ln1_outs, pb_free, pc_free, pd_free, pe_free = emit_ln(
            lambda a_: ybuf[:, a_, :],
            [v_pre1[-1]] + last_x1_readers,
            lambda a_: x1[:, a_, 1:T + 1], pb_free, pc_free, pd_free, pe_free,
            v_scan_prev)
        v_x1 = ln1_outs[-1]

        mix_ms = []
        for i_rk, (mu, omu, off) in enumerate(
                ((sb_mur, sb_omur, 0), (sb_muk, sb_omuk, 8))):
            for a in range(KD):
                def mk_mix(a_, mu_, omu_, off_):
                    def f(e):
                        e.tensor_scalar(out=S1[:, 0, :],
                                        in0=x1[:, a_, 1:T + 1],
                                        scalar1=mu_[:, a_:a_ + 1], scalar2=None,
                                        op0=ALU.mult)
                        return e.scalar_tensor_tensor(
                            out=ucbf[:, off_ + a_, :], in0=x1[:, a_, 0:T],
                            scalar=omu_[:, a_:a_ + 1], in1=S1[:, 0, :],
                            op0=ALU.mult, op1=ALU.add)
                    return f
                mix_ms.append(P.add("V", mk_mix(a, mu, omu, off),
                              deps=[v_x1, v_x1c0, v_yfin, t_wout_last], inc=1))
        v_xb = P.add("V", lambda v: v.tensor_copy(out=xbcol[:], in_=x1[:, :, T:T + 1]),
                     deps=[v_x1], inc=1)

        def ffn_dma(ws, base, g, deps):
            return P.add("Q", (lambda g_, ws_, b_: (lambda q: q.dma_start(
                out=ring_ffn[:, (b_ + g_) % 2, 0:KD, :], in_=ws_[:, g_])))(
                g, ws, base), deps=deps, inc=16, sem=f"QF{(base + g) % 2}")

        q_wr = {g: ffn_dma(wsr, 0, g, [ring_ffn_free[g % 2]]) for g in range(2)}
        q_wk = {}
        q_wv = {}
        rr_ms = []
        for a in range(KD):
            bank = a % 4
            t_ms = P.add("T", (lambda a_, bank_: (lambda t: [
                t.matmul(PA[bank_][:, 0:T], ring_ffn[:, a_ % 2, kt, :],
                         ucbf[:, kt, :], start=(kt == 0), stop=(kt == KD - 1))
                for kt in range(KD)][-1]))(a, bank),
                deps=mix_ms + [q_wr[a], pa_free[bank]], inc=1)
            ring_ffn_free[a % 2] = t_ms
            last_ucbf_T = t_ms
            if a + 2 < KD:
                q_wr[a + 2] = ffn_dma(wsr, 0, a + 2, [t_ms])
            else:
                q_wk[a + 2 - KD] = ffn_dma(wsk, KD, a + 2 - KD, [t_ms])
            ev = P.add("S", (lambda a_, bank_: (lambda s: s.activation(
                out=uc[:, a_, :], in_=PA[bank_][:, 0:T], func=AF.Sigmoid)))(a, bank),
                deps=[t_ms, v_yd, s_uc], inc=1)
            pa_free[bank] = ev
            rr_ms.append(ev)
        kact_ms = []
        for a in range(KD):
            bank = a % 4
            t_ms = P.add("T", (lambda a_, bank_: (lambda t: [
                t.matmul(PA[bank_][:, 0:T], ring_ffn[:, (KD + a_) % 2, kt, :],
                         ucbf[:, 8 + kt, :], start=(kt == 0), stop=(kt == KD - 1))
                for kt in range(KD)][-1]))(a, bank),
                deps=mix_ms + [q_wk[a], pa_free[bank]], inc=1)
            ring_ffn_free[a % 2] = t_ms
            last_ucbf_T = t_ms
            if a + 2 < KD:
                q_wk[a + 2] = ffn_dma(wsk, KD, a + 2, [t_ms])
            else:
                q_wv[a + 2 - KD] = ffn_dma(wsv, 2 * KD, a + 2 - KD, [t_ms])
            ev = P.add("V", (lambda a_, bank_: (lambda v: v.tensor_scalar(
                out=uc[:, 8 + a_, :], in0=PA[bank_][:, 0:T],
                scalar1=0.0, scalar2=None, op0=ALU.max)))(a, bank),
                deps=[t_ms, v_yd], inc=1)
            pa_free[bank] = ev
            kact_ms.append(P.add("S", (lambda a_: (lambda s: s.activation(
                out=S2[:, a_, :], in_=uc[:, 8 + a_, :], func=AF.Square)))(a),
                deps=[ev, s2_free[0], s2_free[1]], inc=1))
        v_pre2 = []
        for a in range(KD):
            bank = a % 4
            t_ms = P.add("T", (lambda a_, bank_: (lambda t: [
                t.matmul(PA[bank_][:, 0:T], ring_ffn[:, a_ % 2, kt, :],
                         S2[:, kt, :], start=(kt == 0), stop=(kt == KD - 1))
                for kt in range(KD)][-1]))(a, bank),
                deps=kact_ms + [q_wv[a], pa_free[bank]], inc=1)
            ring_ffn_free[a % 2] = t_ms
            ffn_v_last = t_ms
            if a + 2 < KD:
                q_wv[a + 2] = ffn_dma(wsv, 2 * KD, a + 2, [t_ms])
            ev = P.add("V", (lambda a_, bank_: (lambda v: v.tensor_tensor(
                out=S3[:, a_, :], in0=PA[bank_][:, 0:T],
                in1=uc[:, a_, :], op=ALU.mult)))(a, bank),
                deps=[t_ms, rr_ms[a], s3_free] + mix_ms, inc=1)
            pa_free[bank] = ev
            v_pre2.append(P.add("G", (lambda a_: (lambda g: g.tensor_tensor(
                out=ybuf[:, 8 + a_, :], in0=S3[:, a_, :],
                in1=x1[:, a_, 1:T + 1], op=ALU.add)))(a), deps=[ev], inc=1))
        s3_free = v_pre2[-1]
        s2_free = [s2_free[0], ffn_v_last]
        last_ucbf_G = ("G", 0)
        last_x1_readers = [v_pre2[-1], v_xb, mix_ms[-1]]

        ln2_outs, pb_free, pc_free, pd_free, pe_free = emit_ln(
            lambda a_: ybuf[:, 8 + a_, :],
            v_pre2 + [ln1_outs[-1], kact_ms[-1]],
            lambda a_: uc[:, a_, :], pb_free, pc_free, pd_free, pe_free,
            ln1_outs[-1])
        q_out = P.add("Q", (lambda j_: (lambda q: q.dma_start(
            out=outTr[:, :, j_ * T:(j_ + 1) * T],
            in_=uc[:, 0:KD, :])))(j), deps=[ln2_outs[-1]], inc=16, sem="QS")
        last_uc_war = q_out
        last_yevac = ln2_outs[-1]
        s1_free = ln2_outs[-1]

    totals = dict(P.cnt)

    block_cm = nc.Block()
    block = block_cm.__enter__()

    @block.sync
    def _(sync):
        P.emit("Q", sync, sems)
        for name, val in totals.items():
            if val > 0:
                sync.wait_ge(sems[name], val)

    @block.vector
    def _(v):
        P.emit("V", v, sems)

    @block.scalar
    def _(s):
        P.emit("S", s, sems)

    @block.gpsimd
    def _(g):
        P.emit("G", g, sems)

    @block.tensor
    def _(t):
        P.emit("T", t, sems)

    block_cm.__exit__(None, None, None)
    return nc


_NC_CACHE = None


def kernel(**inputs):
    global _NC_CACHE
    x = np.asarray(inputs["L0_attn_x"])
    B = x.shape[0]
    if _NC_CACHE is None:
        _NC_CACHE = build_nc()
    nc = _NC_CACHE

    shared = {
        "w_inT": np.ascontiguousarray(np.asarray(inputs["L0_attn_W_in"]).T),
        "convw": np.ascontiguousarray(np.asarray(inputs["L0_attn_conv_w"])[:, 0, :]),
        "convb": np.ascontiguousarray(np.asarray(inputs["L0_attn_conv_b"])),
        "w_xprojT": np.ascontiguousarray(np.asarray(inputs["L0_attn_W_xproj"]).T),
        "w_dtT": np.ascontiguousarray(np.asarray(inputs["L0_attn_W_dt"]).T),
        "b_dt": np.ascontiguousarray(np.asarray(inputs["L0_attn_b_dt"])),
        "A_log": np.ascontiguousarray(np.asarray(inputs["L0_attn_A_log"])),
        "Dvec": np.ascontiguousarray(np.asarray(inputs["L0_attn_D"])),
        "w_outT": np.ascontiguousarray(np.asarray(inputs["L0_attn_W_out"]).T),
        "w_rT": np.ascontiguousarray(np.asarray(inputs["L0_ffn_W_r"]).T),
        "w_kT": np.ascontiguousarray(np.asarray(inputs["L0_ffn_W_k"]).T),
        "w_vT": np.ascontiguousarray(np.asarray(inputs["L0_ffn_W_v"]).T),
        "mu_r": np.ascontiguousarray(np.asarray(inputs["L0_ffn_mu_r"])),
        "mu_k": np.ascontiguousarray(np.asarray(inputs["L0_ffn_mu_k"])),
    }
    in_maps = []
    for b in range(B):
        m = dict(shared)
        m["xT"] = np.ascontiguousarray(x[b].T)
        in_maps.append(m)
    res = run_bass_kernel_spmd(nc, in_maps, core_ids=list(range(B)))
    out = np.stack([np.asarray(res.results[b]["outT"]).T for b in range(B)], axis=0)
    return out.astype(np.float32)


# version 13
# speedup vs baseline: 1.1087x; 1.0261x over previous
import numpy as np
import concourse.bass as bass
import concourse.mybir as mybir
from concourse.bass_utils import run_bass_kernel_spmd

F32 = mybir.dt.float32
BF16 = mybir.dt.bfloat16
AF = mybir.ActivationFunctionType
ALU = mybir.AluOpType

L, DM, H, NS, RK = 4096, 1024, 2048, 16, 64
T = 256
NCH = L // T
E = T + 2
KD = DM // 128
KH = H // 128
NE = 2 * H // 128

ENGS = ("T", "V", "S", "G", "Q")


def bap(ap, reps, inner):
    return bass.AP(ap.tensor, ap.offset, [list(ap.ap[0]), [0, reps], [1, inner]])


def flat(handle_or_ap):
    ap = handle_or_ap.ap() if hasattr(handle_or_ap, "ap") and not isinstance(
        handle_or_ap, bass.AP) else handle_or_ap
    return ap.rearrange("p a t -> p (a t)")


class Prog:
    def __init__(self):
        self.ops = {e: [] for e in ENGS}
        self.cnt = {}
        self.hw = {}

    def snap(self, sem):
        return (sem, self.cnt.get(sem, 0))

    def add(self, eng, fn, deps=(), inc=0, sem=None):
        sem = sem or eng
        best = {}
        for (de, dv) in deps:
            if dv > best.get(de, -1):
                best[de] = dv
        self.ops[eng].append((fn, tuple(best.items()), inc, sem))
        if inc:
            self.cnt[sem] = self.cnt.get(sem, 0) + inc
        return self.snap(sem)

    def compute_needed(self):
        needed = {}
        for eng in ENGS:
            for fn, deps, inc, sem in self.ops[eng]:
                for (de, dv) in deps:
                    if de != eng or eng == "Q":
                        needed.setdefault(de, set()).add(dv)
        for sem, tot in self.cnt.items():
            needed.setdefault(sem, set()).add(tot)
        self.needed = needed

    def emit(self, eng, handle, sems):
        if not hasattr(self, "needed"):
            self.compute_needed()
        run = {}
        pend = {}
        for fn, deps, inc, sem in self.ops[eng]:
            for (de, dv) in deps:
                if dv <= 0:
                    continue
                if de == eng and eng != "Q":
                    continue
                key = (eng, de)
                if self.hw.get(key, -1) >= dv:
                    continue
                self.hw[key] = dv
                handle.wait_ge(sems[de], dv)
            instr = fn(handle)
            if inc:
                assert instr is not None, "milestone op returned no instruction"
                run[sem] = run.get(sem, 0) + inc
                pend[sem] = pend.get(sem, 0) + inc
                if eng in ("V", "S", "G"):
                    if run[sem] in self.needed.get(sem, ()):
                        handle.drain(fusable=True).then_inc(sems[sem], pend[sem])
                        pend[sem] = 0
                else:
                    instr.then_inc(sems[sem], inc)
        for sem, p in pend.items():
            if p:
                handle.drain(fusable=True).then_inc(sems[sem], p)


def build_nc():
    nc = bass.Bass("TRN2", target_bir_lowering=False)

    xT = nc.dram_tensor("xT", [DM, L], F32, kind="ExternalInput").ap()
    w_inT = nc.dram_tensor("w_inT", [DM, 2 * H], F32, kind="ExternalInput").ap()
    convw = nc.dram_tensor("convw", [H, 3], F32, kind="ExternalInput").ap()
    convb = nc.dram_tensor("convb", [H], F32, kind="ExternalInput").ap()
    w_xprojT = nc.dram_tensor("w_xprojT", [H, 96], F32, kind="ExternalInput").ap()
    w_dtT = nc.dram_tensor("w_dtT", [RK, H], F32, kind="ExternalInput").ap()
    b_dt = nc.dram_tensor("b_dt", [H], F32, kind="ExternalInput").ap()
    A_log = nc.dram_tensor("A_log", [NS], F32, kind="ExternalInput").ap()
    Dvec = nc.dram_tensor("Dvec", [H], F32, kind="ExternalInput").ap()
    w_outT = nc.dram_tensor("w_outT", [H, DM], F32, kind="ExternalInput").ap()
    w_rT = nc.dram_tensor("w_rT", [DM, DM], F32, kind="ExternalInput").ap()
    w_kT = nc.dram_tensor("w_kT", [DM, DM], F32, kind="ExternalInput").ap()
    w_vT = nc.dram_tensor("w_vT", [DM, DM], F32, kind="ExternalInput").ap()
    mu_r = nc.dram_tensor("mu_r", [DM], F32, kind="ExternalInput").ap()
    mu_k = nc.dram_tensor("mu_k", [DM], F32, kind="ExternalInput").ap()
    outT = nc.dram_tensor("outT", [DM, L], F32, kind="ExternalOutput").ap()

    wsin = nc.dram_tensor("wsin", [128, NE, 2, KD, 128], BF16, kind="Internal").ap()
    wsout = nc.dram_tensor("wsout", [128, KD, 2, KH, 128], BF16, kind="Internal").ap()
    wsr = nc.dram_tensor("wsr", [128, KD, KD, 128], BF16, kind="Internal").ap()
    wsk = nc.dram_tensor("wsk", [128, KD, KD, 128], BF16, kind="Internal").ap()
    wsv = nc.dram_tensor("wsv", [128, KD, KD, 128], BF16, kind="Internal").ap()
    dscrB = nc.dram_tensor("dscrB", [NS, T], BF16, kind="Internal").ap()
    dscrC = nc.dram_tensor("dscrC", [NS, T], F32, kind="Internal").ap()

    A = nc.alloc_sbuf_tensor
    sb_wxp = A("sb_wxp", [128, KH, 96], F32)
    sb_wdt = A("sb_wdt", [RK, KH, 128], BF16)
    sb_convw = A("sb_convw", [128, KH, 3], F32)
    sb_convb = A("sb_convb", [128, KH], F32)
    sb_bdt = A("sb_bdt", [128, KH], F32)
    sb_D = A("sb_D", [128, KH], F32)
    sb_mur = A("sb_mur", [128, KD], F32)
    sb_muk = A("sb_muk", [128, KD], F32)
    sb_omur = A("sb_omur", [128, KD], F32)
    sb_omuk = A("sb_omuk", [128, KD], F32)
    sb_Arep = A("sb_Arep", [128, NS], F32)
    sb_Alog = A("sb_Alog", [1, NS], F32)
    sb_Aflat = A("sb_Aflat", [1, NS], F32)
    ones_r = A("ones_r", [1, 128], F32)
    ones_c = A("ones_c", [128, 1], F32)
    ident = A("ident", [128, 128], F32)
    ring_in = A("ring_in", [128, 2, 2, KD, 128], BF16)
    ring_out = A("ring_out", [128, 2, 2, KH, 128], BF16)
    ring_ffn = A("ring_ffn", [128, 2, KD, 128], BF16)
    x_ext = A("x_ext", [128, KD, E], F32)
    xbf = A("xbf", [128, KD, E], BF16)
    xlo = A("xlo", [128, KD, E], BF16)
    u_ext = A("u_ext", [128, KH, E], F32)
    sz = A("sz", [128, KH, T], F32)
    uc = A("uc", [128, KH, T], F32)
    ucbf = A("ucbf", [128, KH, T], BF16)
    dtb = A("dtb", [128, KH, T], F32)
    ybuf = A("ybuf", [128, KH, T], F32)
    S1 = A("S1", [128, KH, T], F32)
    S2 = A("S2", [128, KH, T], BF16)
    S3 = A("S3", [128, KH, T], F32)
    Bb = [A("Bba", [128, T], BF16), A("Bbb", [128, T], BF16)]
    Cb = [A("Cba", [128, T], F32), A("Cbb", [128, T], F32)]
    hlast = A("hlast", [128, NS, KH], F32)
    dtpbf = A("dtpbf", [RK, T], BF16)
    BCf32 = A("BCf32", [32, T], F32)
    BCsbB = A("BCsbB", [NS, T], BF16)
    x1 = A("x1", [128, KD, 1 + T], F32)
    xbcol = A("xbcol", [128, KD, 1], F32)
    rowm = A("rowm", [1, T], F32)
    rowv = A("rowv", [1, T], F32)
    rowt = A("rowt", [1, T], F32)
    TMv = u_ext.ap()[:, :, 0:T]
    wtmp = ybuf.ap()[:, :, 0:128]
    wtmpb = ucbf.ap()[:, :, 0:128]

    PA = [nc.alloc_psum_tensor(f"PA{i}", [128, 512], F32) for i in range(4)]
    PB = nc.alloc_psum_tensor("PB", [128, 512], F32)
    PC = nc.alloc_psum_tensor("PC", [128, 512], F32)
    PD = nc.alloc_psum_tensor("PD", [128, 512], F32)
    PE2 = nc.alloc_psum_tensor("PE2", [128, 512], F32)
    PBANKS = PA + [PB, PC, PD, PE2]

    qsems = (["QL", "QP", "QQ", "QX", "QS", "QB1"]
             + [f"QI{i}" for i in range(2)] + [f"QO{i}" for i in range(2)]
             + [f"QF{i}" for i in range(2)]
             + ["QB1C", "QBB0", "QBB1", "QCC0", "QCC1"])
    sems = {e: nc.alloc_semaphore("sem" + e) for e in list(ENGS) + qsems}
    P = Prog()

    def dma(dst, src, slow=False):
        if slow:
            return lambda q: q.dma_start(out=dst, in_=src,
                                         allow_slow_non_contiguous=True)
        return lambda q: q.dma_start(out=dst, in_=src)

    for dst, src in (
        (sb_convw[:], convw.rearrange("(k p) c -> p k c", p=128)),
        (sb_convb[:], convb.rearrange("(k p) -> p k", p=128)),
        (sb_bdt[:], b_dt.rearrange("(k p) -> p k", p=128)),
        (sb_D[:], Dvec.rearrange("(k p) -> p k", p=128)),
        (sb_mur[:], mu_r.rearrange("(k p) -> p k", p=128)),
        (sb_muk[:], mu_k.rearrange("(k p) -> p k", p=128)),
        (sb_Alog[:], bass.AP(A_log.tensor, 0, [[NS, 1], [1, NS]])),
    ):
        P.add("Q", dma(dst, src, slow=True), inc=16, sem="QL")
    q_small = P.snap("QL")

    P.add("V", lambda v: v.memset(ones_r[:], 1.0), inc=1)
    P.add("V", lambda v: v.memset(ones_c[:], 1.0), inc=1)
    P.add("V", lambda v: v.memset(hlast[:], 0.0), inc=1)
    P.add("V", lambda v: v.memset(x_ext[:, :, 0:1], 0.0), inc=1)
    P.add("V", lambda v: v.memset(x1[:, :, 0:1], 0.0), inc=1)
    v_init = P.snap("V")
    P.add("G", lambda g: g.memset(ident[:], 0.0), inc=1)
    P.add("G", lambda g: g.affine_select(
        out=ident[:], in_=ident[:], compare_op=ALU.not_equal, fill=1.0,
        base=0, pattern=[[-1, 128]], channel_multiplier=1), inc=1)
    g_ident = P.snap("G")

    s_ae = P.add("S", lambda s: s.activation(out=sb_Aflat[:], in_=sb_Alog[:],
                 func=AF.Exp), deps=[q_small], inc=1)
    v_aneg = P.add("V", lambda v: v.tensor_scalar(
        out=sb_Aflat[:], in0=sb_Aflat[:], scalar1=-1.0, scalar2=None,
        op0=ALU.mult), deps=[s_ae], inc=1)
    t_arep = P.add("T", lambda t: t.matmul(PB[0:128, 0:NS], ones_r[:], sb_Aflat[:],
                   start=True, stop=True), deps=[v_aneg, v_init], inc=1)
    v_arep = P.add("V", lambda v: v.tensor_copy(out=sb_Arep[:], in_=PB[0:128, 0:NS]),
                   deps=[t_arep], inc=1)
    pb_free = v_arep
    pc_free = ("V", 0)
    pd_free = ("V", 0)
    pe_free = ("V", 0)

    P.add("V", lambda v: v.tensor_scalar(out=sb_omur[:], in0=sb_mur[:],
          scalar1=-1.0, scalar2=1.0, op0=ALU.mult, op1=ALU.add), deps=[q_small], inc=1)
    P.add("V", lambda v: v.tensor_scalar(out=sb_omuk[:], in0=sb_muk[:],
          scalar1=-1.0, scalar2=1.0, op0=ALU.mult, op1=ALU.add), deps=[q_small], inc=1)

    q_wxp = P.add("Q", dma(sb_wxp[:], w_xprojT.rearrange("(k p) e -> p k e", p=128)),
                  inc=16, sem="QL")
    q_wdt = P.add("Q", dma(wtmp[0:RK, :, :],
                  w_dtT.rearrange("r (k p) -> r k p", k=KH)), inc=16, sem="QL")
    v_wdt = P.add("V", lambda v: v.tensor_copy(out=sb_wdt[:], in_=wtmp[0:RK, :, :]),
                  deps=[q_wdt], inc=1)

    pieces = []
    w_inT_r = w_inT.rearrange("(k p) e -> p k e", p=128)
    for et in range(NE):
        pieces.append((w_inT_r[:, :, et * 128:(et + 1) * 128], KD, wsin[:, et], True))
    w_outT_r = w_outT.rearrange("(k p) e -> p k e", p=128)
    for a in range(KD):
        pieces.append((w_outT_r[:, :, a * 128:(a + 1) * 128], KH, wsout[:, a], True))
    for (w, ws) in ((w_rT, wsr), (w_kT, wsk), (w_vT, wsv)):
        w_r = w.rearrange("(k p) e -> p k e", p=128)
        for a in range(KD):
            pieces.append((w_r[:, :, a * 128:(a + 1) * 128], KD, ws[:, a], False))

    prev_v = v_wdt
    prev_store = ("QQ", 0)
    for (src, ntile, dst, lo) in pieces:
        ld = P.add("Q", dma(wtmp[:, 0:ntile, :], src), deps=[prev_v], inc=16, sem="QP")
        if lo:
            P.add("V", (lambda nt: (lambda v: v.tensor_copy(
                out=wtmpb[:, 0:nt, :], in_=wtmp[:, 0:nt, :])))(ntile),
                deps=[ld, prev_store], inc=1)
            h_v = P.snap("V")
            hst = P.add("Q", (lambda d, nt: (lambda q: q.dma_start(
                out=d[:, 0], in_=wtmpb[:, 0:nt, :])))(dst, ntile),
                deps=[h_v], inc=16, sem="QQ")
            P.add("V", (lambda nt: (lambda v: v.tensor_tensor(
                out=wtmpb[:, 0:nt, :], in0=wtmp[:, 0:nt, :],
                in1=wtmpb[:, 0:nt, :], op=ALU.subtract)))(ntile),
                deps=[hst], inc=1)
            prev_v = P.snap("V")
            prev_store = P.add("Q", (lambda d, nt: (lambda q: q.dma_start(
                out=d[:, 1], in_=wtmpb[:, 0:nt, :])))(dst, ntile),
                deps=[prev_v], inc=16, sem="QQ")
        else:
            P.add("V", (lambda nt: (lambda v: v.tensor_copy(
                out=wtmpb[:, 0:nt, :], in_=wtmp[:, 0:nt, :])))(ntile),
                deps=[ld, prev_store], inc=1)
            prev_v = P.snap("V")
            prev_store = P.add("Q", (lambda d, nt: (lambda q: q.dma_start(
                out=d, in_=wtmpb[:, 0:nt, :])))(dst, ntile),
                deps=[prev_v], inc=16, sem="QQ")
    q_wbig = P.snap("QQ")

    last_x_reader = ("V", 0)
    last_uext_V = ("V", 0)
    last_uext_T = ("T", 0)
    last_sz_reader = ("V", 0)
    last_dt_readers = []
    last_yevac = ("V", 0)
    last_x1_readers = []
    last_uc_war = ("QS", 0)
    last_ucbf_T = ("T", 0)
    last_ucbf_G = ("G", 0)
    last_xbf_T = ("T", 0)
    last_bcsb_Q = ("QB1", 0)
    last_dscr_Q = []
    s1_free = ("V", 0)
    s2_free = [("V", 0), ("T", 0)]
    s3_free = ("V", 0)
    bb_free = [("V", 0), ("V", 0)]
    cb_free = [("V", 0), ("V", 0)]
    pa_free = [v_init] * 4
    ring_in_free = [("T", 0)] * 2
    ring_out_free = [("T", 0)] * 2
    ring_ffn_free = [("T", 0)] * 2

    xTr = xT.rearrange("(k p) t -> p k t", p=128)
    outTr = outT.rearrange("(k p) t -> p k t", p=128)

    def emit_ln(src_of, pre_ms, dst_of, pbf, pcf, pdf, pef, s1f):
        t_sx = P.add("T", lambda t: [t.matmul(
            PD[0:1, 0:T], ones_c[:], src_of(a), start=(a == 0),
            stop=(a == KD - 1)) for a in range(KD)][-1],
            deps=pre_ms + [pdf], inc=1)
        sq_ms = []
        for a in range(KD):
            sq_ms.append(P.add("S", (lambda a_: (lambda s: s.activation(
                out=S1[:, a_, :], in_=src_of(a_), func=AF.Square)))(a),
                deps=pre_ms + [s1f], inc=1))
        t_sq = P.add("T", lambda t: [t.matmul(
            PE2[0:1, 0:T], ones_c[:], S1[:, a, :], start=(a == 0),
            stop=(a == KD - 1)) for a in range(KD)][-1],
            deps=sq_ms + [pef], inc=1)
        P.add("V", lambda v: v.tensor_scalar(out=rowm[:], in0=PD[0:1, 0:T],
              scalar1=1.0 / DM, scalar2=None, op0=ALU.mult), deps=[t_sx], inc=1)
        npdf = P.snap("V")
        P.add("V", lambda v: v.tensor_scalar(out=rowv[:], in0=PE2[0:1, 0:T],
              scalar1=1.0 / DM, scalar2=1e-5, op0=ALU.mult, op1=ALU.add),
              deps=[t_sq], inc=1)
        npef = P.snap("V")
        P.add("V", lambda v: v.tensor_tensor(out=rowt[:], in0=rowm[:],
              in1=rowm[:], op=ALU.mult), inc=1)
        v_var = P.add("V", lambda v: v.tensor_tensor(out=rowv[:], in0=rowv[:],
              in1=rowt[:], op=ALU.subtract), inc=1)
        P.add("S", lambda s: s.activation(out=rowt[:], in_=rowv[:],
              func=AF.Ln), deps=[v_var], inc=1)
        s_rstd = P.add("S", lambda s: s.activation(out=rowt[:], in_=rowt[:],
              func=AF.Exp, scale=-0.5), inc=1)
        t_mrep = P.add("T", lambda t: t.matmul(PB[:, 0:T], ones_r[:], rowm[:],
              start=True, stop=True), deps=[v_var, pbf], inc=1)
        t_rrep = P.add("T", lambda t: t.matmul(PC[:, 0:T], ones_r[:], rowt[:],
              start=True, stop=True), deps=[s_rstd, pcf], inc=1)
        outs = []
        npbf = None
        for a in range(KD):
            P.add("V", (lambda a_: (lambda v: v.tensor_tensor(
                out=S1[:, a_, :], in0=src_of(a_), in1=PB[:, 0:T],
                op=ALU.subtract)))(a), deps=[t_mrep], inc=1)
            if a == KD - 1:
                npbf = P.snap("V")
            outs.append(P.add("V", (lambda a_: (lambda v: v.tensor_tensor(
                out=dst_of(a_), in0=S1[:, a_, :], in1=PC[:, 0:T],
                op=ALU.mult)))(a), deps=[t_rrep], inc=1))
        return outs, npbf, outs[-1], npdf, npef

    for j in range(NCH):
        s0 = j * T

        if j == 0:
            q_x = P.add("Q", dma(x_ext[:, :, 1:E], xTr[:, :, 0:T + 1]),
                        deps=[last_x_reader, q_wbig], inc=16, sem="QX")
        elif j == NCH - 1:
            q_x = P.add("Q", dma(x_ext[:, :, 0:E - 1], xTr[:, :, s0 - 1:s0 + T]),
                        deps=[last_x_reader], inc=16, sem="QX")
        else:
            q_x = P.add("Q", dma(x_ext[:, :, 0:E], xTr[:, :, s0 - 1:s0 + T + 1]),
                        deps=[last_x_reader], inc=16, sem="QX")
        if j == NCH - 1:
            P.add("V", lambda v: v.memset(x_ext[:, :, E - 1:E], 0.0),
                  deps=[last_x_reader], inc=1)
        P.add("V", lambda v: v.tensor_copy(out=xbf[:], in_=x_ext[:]),
              deps=[q_x, last_xbf_T], inc=1)
        v_xbf = P.add("V", lambda v: v.scalar_tensor_tensor(
            out=xlo[:], in0=xbf[:], scalar=-1.0, in1=x_ext[:],
            op0=ALU.mult, op1=ALU.add), deps=[q_x, last_xbf_T], inc=1)

        q_wing = {}
        for g in range(2):
            q_wing[g] = P.add("Q", (lambda g_: (lambda q: q.dma_start(
                out=ring_in[:, g_ % 2], in_=wsin[:, g_])))(g),
                deps=[ring_in_free[g % 2]], inc=16, sem=f"QI{g % 2}")
        v_uevac = []
        s_zevac = []
        t_xz_last = None
        for et in range(NE):
            bank = et % 4
            slot = et % 2

            def mk_xz(et_, bank_, slot_):
                def f(t):
                    ins = None
                    for dk in range(KD):
                        ins = t.matmul(PA[bank_][:, 0:E],
                                       ring_in[:, slot_, 0, dk, :],
                                       xbf[:, dk, :], start=(dk == 0), stop=False)
                    for dk in range(KD):
                        ins = t.matmul(PA[bank_][:, 0:E],
                                       ring_in[:, slot_, 0, dk, :],
                                       xlo[:, dk, :], start=False, stop=False)
                    for dk in range(KD):
                        ins = t.matmul(PA[bank_][:, 0:E],
                                       ring_in[:, slot_, 1, dk, :],
                                       xbf[:, dk, :], start=False,
                                       stop=(dk == KD - 1))
                    return ins
                return f
            t_ms = P.add("T", mk_xz(et, bank, slot),
                         deps=[v_xbf, q_wing[et], pa_free[bank]], inc=1)
            ring_in_free[slot] = t_ms
            t_xz_last = t_ms
            if et + 2 < NE:
                q_wing[et + 2] = P.add("Q", (lambda g_: (lambda q: q.dma_start(
                    out=ring_in[:, g_ % 2], in_=wsin[:, g_])))(et + 2),
                    deps=[t_ms], inc=16, sem=f"QI{(et + 2) % 2}")
            if et < KH:
                deps = [t_ms]
                if et == 0:
                    deps += [last_uext_V, last_uext_T]
                    if j == 0:
                        deps += [q_wbig]
                ev = P.add("V", (lambda et_, bank_: (lambda v: v.tensor_copy(
                    out=u_ext[:, et_, :], in_=PA[bank_][:, 0:E])))(et, bank),
                    deps=deps, inc=1)
                v_uevac.append(ev)
                pa_free[bank] = ev
            else:
                deps = [t_ms]
                if et == KH:
                    deps += [last_sz_reader]
                ev = P.add("S", (lambda et_, bank_: (lambda s: s.activation(
                    out=sz[:, et_ - KH, :], in_=PA[bank_][:, 1:T + 1],
                    func=AF.Silu)))(et, bank), deps=deps, inc=1)
                s_zevac.append(ev)
                pa_free[bank] = ev
        last_xbf_T = t_xz_last

        conv_V = None
        for ct in range(KH):
            deps = [v_uevac[ct]]
            if ct == 0:
                deps = deps + last_dt_readers + [last_uc_war]

            def mk_conv(ct_):
                def f(e):
                    e.tensor_scalar(out=dtb[:, ct_, :], in0=u_ext[:, ct_, 0:T],
                                    scalar1=sb_convw[:, ct_, 0:1],
                                    scalar2=sb_convb[:, ct_:ct_ + 1],
                                    op0=ALU.mult, op1=ALU.add)
                    e.scalar_tensor_tensor(out=dtb[:, ct_, :],
                                           in0=u_ext[:, ct_, 1:T + 1],
                                           scalar=sb_convw[:, ct_, 1:2],
                                           in1=dtb[:, ct_, :],
                                           op0=ALU.mult, op1=ALU.add)
                    return e.scalar_tensor_tensor(
                        out=dtb[:, ct_, :], in0=u_ext[:, ct_, 2:T + 2],
                        scalar=sb_convw[:, ct_, 2:3], in1=dtb[:, ct_, :],
                        op0=ALU.mult, op1=ALU.add)
                return f
            conv_V = P.add("V", mk_conv(ct), deps=deps, inc=1)
        s_uc = P.add("S", lambda s: s.activation(out=flat(uc), in_=flat(dtb),
                     func=AF.Silu), deps=[conv_V, last_uc_war], inc=1)
        v_ucbf = P.add("V", lambda v: v.tensor_copy(out=ucbf[:], in_=uc[:]),
                       deps=[s_uc, last_ucbf_T, last_ucbf_G] +
                       ([q_wbig] if j == 0 else []), inc=1)

        def mk_xdbl(t):
            ins = None
            for kt in range(KH):
                ins = t.matmul(PB[0:96, 0:T], sb_wxp[:, kt, :], uc[:, kt, :],
                               start=(kt == 0), stop=(kt == KH - 1))
            return ins
        t_xdbl = P.add("T", mk_xdbl, deps=[s_uc, q_wxp, pb_free], inc=1)
        v_dtp = P.add("V", lambda v: v.tensor_copy(out=dtpbf[:], in_=PB[0:RK, 0:T]),
                      deps=[t_xdbl], inc=1)
        P.add("V", lambda v: v.tensor_copy(out=BCf32[:], in_=PB[64:96, 0:T]),
              deps=[t_xdbl, last_bcsb_Q], inc=1)
        v_bc = P.add("V", lambda v: v.tensor_copy(out=BCsbB[:], in_=BCf32[0:NS, :]),
                     deps=[], inc=1)
        pb_free = v_bc
        q_bc1 = P.add("Q", dma(dscrB[:], BCsbB[:]),
                      deps=[v_bc] + last_dscr_Q, inc=16, sem="QB1")
        q_bc1c = P.add("Q", dma(dscrC[:], BCf32[NS:32, :]),
                       deps=[v_bc] + last_dscr_Q, inc=16, sem="QB1C")
        last_bcsb_Q = q_bc1c

        ev_last = None
        for ht in range(KH):
            bank = ht % 4
            t_ms = P.add("T", (lambda ht_, bank_: (lambda t: t.matmul(
                PA[bank_][:, 0:T], sb_wdt[:, ht_, :], dtpbf[:],
                start=True, stop=True)))(ht, bank),
                deps=[v_dtp, v_wdt, pa_free[bank]], inc=1)
            deps = [t_ms]
            if ht == 0:
                deps += [last_yevac]
            ev_last = P.add("V", (lambda ht_, bank_: (lambda v: v.tensor_scalar(
                out=ybuf[:, ht_, :], in0=PA[bank_][:, 0:T],
                scalar1=sb_bdt[:, ht_:ht_ + 1], scalar2=None, op0=ALU.add)))(ht, bank),
                deps=deps, inc=1)
            pa_free[bank] = ev_last
        v_dtpre = ev_last
        P.add("V", lambda v: v.tensor_scalar(out=flat(dtb), in0=flat(ybuf),
              scalar1=0.0, scalar2=None, op0=ALU.max),
              deps=[v_dtpre, conv_V, s_uc], inc=1)
        P.add("V", lambda v: v.tensor_scalar(out=flat(S1), in0=flat(ybuf),
              scalar1=0.0, scalar2=None, op0=ALU.min),
              deps=[v_dtpre, s1_free], inc=1)
        v_mn = P.snap("V")
        P.add("S", lambda s: s.activation(out=flat(S2), in_=flat(S1),
              func=AF.Exp), deps=[v_mn, s2_free[0], s2_free[1]], inc=1)
        P.add("S", lambda s: s.activation(out=flat(S1), in_=flat(S2),
              func=AF.Ln, bias=1.0), inc=1)
        s_spl = P.snap("S")
        s2_free = [s_spl, s2_free[1]]
        v_dt = P.add("V", lambda v: v.tensor_tensor(out=flat(dtb), in0=flat(dtb),
              in1=flat(S1), op=ALU.add), deps=[s_spl], inc=1)

        last_dt_readers = []
        last_dscr_Q = []
        t_imm_last = None
        v_scan_prev = ("V", 0)
        da_free = [("V", 0), ("V", 0)]
        for n in range(NS):
            par = n % 2
            q_bb = P.add("Q", (lambda n_, par_: (lambda q: q.dma_start(
                out=Bb[par_][:],
                in_=bass.AP(dscrB.tensor, n_ * T, [[0, 128], [1, T]]))))(n, par),
                deps=[q_bc1, bb_free[par]], inc=16, sem=f"QBB{par}")
            q_cc = P.add("Q", (lambda n_, par_: (lambda q: q.dma_start(
                out=Cb[par_][:],
                in_=bass.AP(dscrC.tensor, n_ * T, [[0, 128], [1, T]]))))(n, par),
                deps=[q_bc1c, cb_free[par]], inc=16, sem=f"QCC{par}")
            last_dscr_Q += [q_bb, q_cc]
            dAbuf = S1
            s_da = P.add("S", (lambda n_, b_: (lambda s: s.activation(
                out=flat(b_), in_=flat(dtb), func=AF.Exp,
                scale=sb_Arep[:, n_:n_ + 1])))(n, dAbuf),
                deps=[v_dt, v_arep, v_scan_prev], inc=1)
            last_dt_readers.append(s_da)
            v_r = P.add("V", (lambda par_: (lambda v: v.tensor_tensor(
                out=S2[:], in0=ucbf[:], in1=bap(Bb[par_][:], KH, T),
                op=ALU.mult)))(par),
                deps=[q_bb, v_ucbf, s2_free[0], s2_free[1]], inc=1)
            s2_free = [v_r, s2_free[1]]

            def mk_scan(n_, b_):
                def f(v):
                    ins = None
                    for k in range(KH):
                        ins = v.tensor_tensor_scan(
                            out=S3[:, k, :], data0=b_[:, k, :], data1=S2[:, k, :],
                            initial=hlast[:, n_, k:k + 1],
                            op0=ALU.mult, op1=ALU.add)
                    return ins
                return f
            v_scan = P.add("V", mk_scan(n, dAbuf), deps=[s_da, s3_free], inc=1)
            v_scan_prev = v_scan
            da_free[n % 2] = v_scan
            bb_free[par] = v_scan
            P.add("V", (lambda n_: (lambda v: v.tensor_copy(
                out=hlast[:, n_, :],
                in_=S3[:, :, T - 1:T].rearrange("p a o -> p (a o)"))))(n), inc=1)
            v_term = P.add("V", (lambda par_: (lambda v: v.tensor_tensor(
                out=TMv, in0=S3[:], in1=bap(Cb[par_][:], KH, T),
                op=ALU.mult)))(par),
                deps=[q_cc, last_uext_T, conv_V], inc=1)
            cb_free[par] = v_term
            s3_free = v_term

            def mk_imm(n_):
                def f(t):
                    ins = None
                    for k in range(8):
                        ins = t.matmul(
                            PBANKS[k][:, 0:512], ident[:],
                            bass.AP(TMv.tensor, 2 * k * E,
                                    [[KH * E, 128], [E, 2], [1, T]]),
                            start=(n_ == 0), stop=(n_ == NS - 1),
                            skip_group_check=True)
                    return ins
                return f
            imm_deps = [v_term, g_ident]
            if n == 0:
                imm_deps += list(pa_free) + [pb_free, pc_free, pd_free, pe_free]
            t_imm = P.add("T", mk_imm(n), deps=imm_deps, inc=1)
            last_uext_T = t_imm
            t_imm_last = t_imm
        for k in range(8):
            ev = P.add("V", (lambda k_: (lambda v: v.tensor_copy(
                out=ybuf[:, 2 * k_:2 * k_ + 2, :].rearrange("p a t -> p (a t)"),
                in_=PBANKS[k_][:, 0:512])))(k),
                deps=[t_imm_last, last_dt_readers[-1]], inc=1)
            if k < 4:
                pa_free[k] = ev
        last_yevac = P.snap("V")
        pb_free = pc_free = pd_free = pe_free = last_yevac
        last_uext_V = last_yevac

        v_yd = None
        for ct in range(KH):
            v_yd = P.add("V", (lambda ct_: (lambda v: v.scalar_tensor_tensor(
                out=ybuf[:, ct_, :], in0=uc[:, ct_, :],
                scalar=sb_D[:, ct_:ct_ + 1], in1=ybuf[:, ct_, :],
                op0=ALU.mult, op1=ALU.add)))(ct), deps=[last_yevac, s_uc], inc=1)
        P.add("V", lambda v: v.tensor_tensor(
            out=flat(ybuf), in0=flat(ybuf), in1=flat(sz), op=ALU.mult),
            deps=[v_yd, s_zevac[-1]], inc=1)
        v_yfin = P.add("V", lambda v: v.tensor_copy(out=ucbf[:], in_=ybuf[:]),
                       deps=[last_ucbf_G, last_ucbf_T, t_xdbl], inc=1)
        v_ylo = P.add("V", lambda v: v.scalar_tensor_tensor(
            out=flat(S2), in0=flat(ucbf), scalar=-1.0, in1=flat(ybuf),
            op0=ALU.mult, op1=ALU.add), deps=[s2_free[0], s2_free[1]], inc=1)
        s2_free = [v_ylo, s2_free[1]]
        last_sz_reader = v_yfin

        q_wout = {}
        for g in range(2):
            q_wout[g] = P.add("Q", (lambda g_: (lambda q: q.dma_start(
                out=ring_out[:, g_ % 2], in_=wsout[:, g_])))(g),
                deps=[ring_out_free[g % 2]], inc=16, sem=f"QO{g % 2}")
        v_pre1 = []
        t_wout_last = None
        for a in range(KD):
            bank = a % 4
            slot = a % 2

            def mk_wout(a_, bank_, slot_):
                def f(t):
                    ins = None
                    for kt in range(KH):
                        ins = t.matmul(PA[bank_][:, 0:T], ring_out[:, slot_, 0, kt, :],
                                       ucbf[:, kt, :], start=(kt == 0), stop=False)
                    for kt in range(KH):
                        ins = t.matmul(PA[bank_][:, 0:T], ring_out[:, slot_, 0, kt, :],
                                       S2[:, kt, :], start=False, stop=False)
                    for kt in range(KH):
                        ins = t.matmul(PA[bank_][:, 0:T], ring_out[:, slot_, 1, kt, :],
                                       ucbf[:, kt, :], start=False,
                                       stop=(kt == KH - 1))
                    return ins
                return f
            t_ms = P.add("T", mk_wout(a, bank, slot),
                         deps=[v_yfin, v_ylo, q_wout[a], pa_free[bank]], inc=1)
            ring_out_free[slot] = t_ms
            t_wout_last = t_ms
            if a + 2 < KD:
                q_wout[a + 2] = P.add("Q", (lambda g_: (lambda q: q.dma_start(
                    out=ring_out[:, g_ % 2], in_=wsout[:, g_])))(a + 2),
                    deps=[t_ms], inc=16, sem=f"QO{a % 2}")
            ev = P.add("V", (lambda a_, bank_: (lambda v: v.scalar_tensor_tensor(
                out=ybuf[:, a_, :], in0=x_ext[:, a_, 1:T + 1], scalar=2.0,
                in1=PA[bank_][:, 0:T], op0=ALU.mult, op1=ALU.add)))(a, bank),
                deps=[t_ms, q_x], inc=1)
            pa_free[bank] = ev
            v_pre1.append(ev)
        last_x_reader = v_pre1[-1]
        last_ucbf_T = t_wout_last
        s2_free = [s2_free[0], t_wout_last]

        if j > 0:
            P.add("V", lambda v: v.tensor_copy(out=x1[:, :, 0:1], in_=xbcol[:]),
                  deps=[v_pre1[-1]] + last_x1_readers, inc=1)
        v_x1c0 = P.snap("V")
        ln1_outs, pb_free, pc_free, pd_free, pe_free = emit_ln(
            lambda a_: ybuf[:, a_, :],
            [v_pre1[-1]] + last_x1_readers,
            lambda a_: x1[:, a_, 1:T + 1], pb_free, pc_free, pd_free, pe_free,
            v_scan_prev)
        v_x1 = ln1_outs[-1]

        mix_ms = []
        for i_rk, (mu, omu, off) in enumerate(
                ((sb_mur, sb_omur, 0), (sb_muk, sb_omuk, 8))):
            for a in range(KD):
                def mk_mix(a_, mu_, omu_, off_):
                    def f(e):
                        e.tensor_scalar(out=S1[:, 0, :],
                                        in0=x1[:, a_, 1:T + 1],
                                        scalar1=mu_[:, a_:a_ + 1], scalar2=None,
                                        op0=ALU.mult)
                        return e.scalar_tensor_tensor(
                            out=ucbf[:, off_ + a_, :], in0=x1[:, a_, 0:T],
                            scalar=omu_[:, a_:a_ + 1], in1=S1[:, 0, :],
                            op0=ALU.mult, op1=ALU.add)
                    return f
                mix_ms.append(P.add("V", mk_mix(a, mu, omu, off),
                              deps=[v_x1, v_x1c0, v_yfin, t_wout_last], inc=1))
        v_xb = P.add("V", lambda v: v.tensor_copy(out=xbcol[:], in_=x1[:, :, T:T + 1]),
                     deps=[v_x1], inc=1)

        def ffn_dma(ws, base, g, deps):
            return P.add("Q", (lambda g_, ws_, b_: (lambda q: q.dma_start(
                out=ring_ffn[:, (b_ + g_) % 2, 0:KD, :], in_=ws_[:, g_])))(
                g, ws, base), deps=deps, inc=16, sem=f"QF{(base + g) % 2}")

        q_wr = {g: ffn_dma(wsr, 0, g, [ring_ffn_free[g % 2]]) for g in range(2)}
        q_wk = {}
        q_wv = {}
        rr_ms = []
        for a in range(KD):
            bank = a % 4
            t_ms = P.add("T", (lambda a_, bank_: (lambda t: [
                t.matmul(PA[bank_][:, 0:T], ring_ffn[:, a_ % 2, kt, :],
                         ucbf[:, kt, :], start=(kt == 0), stop=(kt == KD - 1))
                for kt in range(KD)][-1]))(a, bank),
                deps=mix_ms + [q_wr[a], pa_free[bank]], inc=1)
            ring_ffn_free[a % 2] = t_ms
            last_ucbf_T = t_ms
            if a + 2 < KD:
                q_wr[a + 2] = ffn_dma(wsr, 0, a + 2, [t_ms])
            else:
                q_wk[a + 2 - KD] = ffn_dma(wsk, KD, a + 2 - KD, [t_ms])
            ev = P.add("S", (lambda a_, bank_: (lambda s: s.activation(
                out=uc[:, a_, :], in_=PA[bank_][:, 0:T], func=AF.Sigmoid)))(a, bank),
                deps=[t_ms, v_yd, s_uc], inc=1)
            pa_free[bank] = ev
            rr_ms.append(ev)
        kact_ms = []
        for a in range(KD):
            bank = a % 4
            t_ms = P.add("T", (lambda a_, bank_: (lambda t: [
                t.matmul(PA[bank_][:, 0:T], ring_ffn[:, (KD + a_) % 2, kt, :],
                         ucbf[:, 8 + kt, :], start=(kt == 0), stop=(kt == KD - 1))
                for kt in range(KD)][-1]))(a, bank),
                deps=mix_ms + [q_wk[a], pa_free[bank]], inc=1)
            ring_ffn_free[a % 2] = t_ms
            last_ucbf_T = t_ms
            if a + 2 < KD:
                q_wk[a + 2] = ffn_dma(wsk, KD, a + 2, [t_ms])
            else:
                q_wv[a + 2 - KD] = ffn_dma(wsv, 2 * KD, a + 2 - KD, [t_ms])
            ev = P.add("V", (lambda a_, bank_: (lambda v: v.tensor_scalar(
                out=uc[:, 8 + a_, :], in0=PA[bank_][:, 0:T],
                scalar1=0.0, scalar2=None, op0=ALU.max)))(a, bank),
                deps=[t_ms, v_yd], inc=1)
            pa_free[bank] = ev
            kact_ms.append(P.add("S", (lambda a_: (lambda s: s.activation(
                out=S2[:, a_, :], in_=uc[:, 8 + a_, :], func=AF.Square)))(a),
                deps=[ev, s2_free[0], s2_free[1]], inc=1))
        v_pre2 = []
        for a in range(KD):
            bank = a % 4
            t_ms = P.add("T", (lambda a_, bank_: (lambda t: [
                t.matmul(PA[bank_][:, 0:T], ring_ffn[:, a_ % 2, kt, :],
                         S2[:, kt, :], start=(kt == 0), stop=(kt == KD - 1))
                for kt in range(KD)][-1]))(a, bank),
                deps=kact_ms + [q_wv[a], pa_free[bank]], inc=1)
            ring_ffn_free[a % 2] = t_ms
            ffn_v_last = t_ms
            if a + 2 < KD:
                q_wv[a + 2] = ffn_dma(wsv, 2 * KD, a + 2, [t_ms])
            ev = P.add("V", (lambda a_, bank_: (lambda v: v.tensor_tensor(
                out=S3[:, a_, :], in0=PA[bank_][:, 0:T],
                in1=uc[:, a_, :], op=ALU.mult)))(a, bank),
                deps=[t_ms, rr_ms[a], s3_free] + mix_ms, inc=1)
            pa_free[bank] = ev
            v_pre2.append(P.add("G", (lambda a_: (lambda g: g.tensor_tensor(
                out=ybuf[:, 8 + a_, :], in0=S3[:, a_, :],
                in1=x1[:, a_, 1:T + 1], op=ALU.add)))(a), deps=[ev], inc=1))
        s3_free = v_pre2[-1]
        s2_free = [s2_free[0], ffn_v_last]
        last_ucbf_G = ("G", 0)
        last_x1_readers = [v_pre2[-1], v_xb, mix_ms[-1]]

        ln2_outs, pb_free, pc_free, pd_free, pe_free = emit_ln(
            lambda a_: ybuf[:, 8 + a_, :],
            v_pre2 + [ln1_outs[-1], kact_ms[-1]],
            lambda a_: uc[:, a_, :], pb_free, pc_free, pd_free, pe_free,
            ln1_outs[-1])
        q_out = P.add("Q", (lambda j_: (lambda q: q.dma_start(
            out=outTr[:, :, j_ * T:(j_ + 1) * T],
            in_=uc[:, 0:KD, :])))(j), deps=[ln2_outs[-1]], inc=16, sem="QS")
        last_uc_war = q_out
        last_yevac = ln2_outs[-1]
        s1_free = ln2_outs[-1]

    totals = dict(P.cnt)

    block_cm = nc.Block()
    block = block_cm.__enter__()

    @block.sync
    def _(sync):
        P.emit("Q", sync, sems)
        for name, val in totals.items():
            if val > 0:
                sync.wait_ge(sems[name], val)

    @block.vector
    def _(v):
        P.emit("V", v, sems)

    @block.scalar
    def _(s):
        P.emit("S", s, sems)

    @block.gpsimd
    def _(g):
        P.emit("G", g, sems)

    @block.tensor
    def _(t):
        P.emit("T", t, sems)

    block_cm.__exit__(None, None, None)
    return nc


_NC_CACHE = None


def kernel(**inputs):
    global _NC_CACHE
    x = np.asarray(inputs["L0_attn_x"])
    B = x.shape[0]
    if _NC_CACHE is None:
        _NC_CACHE = build_nc()
    nc = _NC_CACHE

    shared = {
        "w_inT": np.ascontiguousarray(np.asarray(inputs["L0_attn_W_in"]).T),
        "convw": np.ascontiguousarray(np.asarray(inputs["L0_attn_conv_w"])[:, 0, :]),
        "convb": np.ascontiguousarray(np.asarray(inputs["L0_attn_conv_b"])),
        "w_xprojT": np.ascontiguousarray(np.asarray(inputs["L0_attn_W_xproj"]).T),
        "w_dtT": np.ascontiguousarray(np.asarray(inputs["L0_attn_W_dt"]).T),
        "b_dt": np.ascontiguousarray(np.asarray(inputs["L0_attn_b_dt"])),
        "A_log": np.ascontiguousarray(np.asarray(inputs["L0_attn_A_log"])),
        "Dvec": np.ascontiguousarray(np.asarray(inputs["L0_attn_D"])),
        "w_outT": np.ascontiguousarray(np.asarray(inputs["L0_attn_W_out"]).T),
        "w_rT": np.ascontiguousarray(np.asarray(inputs["L0_ffn_W_r"]).T),
        "w_kT": np.ascontiguousarray(np.asarray(inputs["L0_ffn_W_k"]).T),
        "w_vT": np.ascontiguousarray(np.asarray(inputs["L0_ffn_W_v"]).T),
        "mu_r": np.ascontiguousarray(np.asarray(inputs["L0_ffn_mu_r"])),
        "mu_k": np.ascontiguousarray(np.asarray(inputs["L0_ffn_mu_k"])),
    }
    in_maps = []
    for b in range(B):
        m = dict(shared)
        m["xT"] = np.ascontiguousarray(x[b].T)
        in_maps.append(m)
    res = run_bass_kernel_spmd(nc, in_maps, core_ids=list(range(B)))
    out = np.stack([np.asarray(res.results[b]["outT"]).T for b in range(B)], axis=0)
    return out.astype(np.float32)
